# version 04
# baseline (speedup 1.0000x reference)
import contextlib
import numpy as np
import concourse.bass as bass
import concourse.mybir as mybir
from concourse.bass_utils import run_bass_kernel_spmd

F32 = mybir.dt.float32
I32 = mybir.dt.int32
BF16 = mybir.dt.bfloat16
OP = mybir.AluOpType
AF = mybir.ActivationFunctionType
AX = mybir.AxisListType

T = 2048
D = 2048
C = 512
NCH = T // C
NF = D // 128
LAT = 1024
NH, NKV, HD, KD = 16, 4, 128, 64
NPAIR = 8
PAIR_A = [0, 1, 2, 3, 8, 9, 10, 11]

PHASES = ("entry", "prep", "ffn1", "attn", "ffn2", "exit")
DEBUG = False
ATTN_NCH = NCH
ATTN_NPAIR = NPAIR
ATTN_INNER = True
ATTN_LVL = 6
ATTN_WO = True

ENGS = ("sync", "tensor", "vector", "scalar", "gpsimd")
ESEM = {"sync": None, "tensor": "smm", "vector": "sdv", "scalar": "sac",
        "gpsimd": "spl"}


class Buf:
    __slots__ = ("w", "r", "wvi")

    def __init__(self):
        self.w = None
        self.r = []
        self.wvi = -10


class B:

    def __init__(self):
        self.ops = {e: [] for e in ENGS}
        self.src = {e: [] for e in ENGS}
        self.cnt = {}
        self.sems = {}
        self.last_vec = None
        self.sep_ap = None
        self.vec_idx = 0

    def _bump(self, s, n):
        self.cnt[s] = self.cnt.get(s, 0) + n
        return self.cnt[s]

    def _pad_src(self, eng):
        pass

    def _waits(self, eng, ins, outs):
        evs = []
        for b in ins:
            if b.w:
                evs.append(b.w)
        for b in outs:
            if b.w:
                evs.append(b.w)
            evs.extend(b.r)
        best = {}
        for s, v, src in evs:
            if src == eng and src is not None:
                continue
            if s not in best or best[s] < v:
                best[s] = v
        for s, v in best.items():
            self.src[eng].append("wait")
            self.ops[eng].append(lambda e, sm=s, vv=v: e.wait_ge(self.sems[sm], vv))

    def op(self, eng, meth, outs, ins, *args, **kw):
        if eng == "vector" and self.sep_ap is not None and meth != "_sep":
            need = 0
            for b in ins:
                gap = self.vec_idx - b.wvi
                if 0 < gap < 3:
                    need = max(need, 3 - gap)
            for _ in range(need):
                self.op("vector", "_sep", [], [])
        if meth == "_sep":
            eng, meth, args, kw = "vector", "tensor_copy",                 (self.sep_ap, self.sep_ap), {}
        if eng == "vector":
            self.last_vec = meth
        self._waits(eng, ins, outs)
        s = ESEM[eng]
        v = self._bump(s, 1)
        import sys as _s
        fr = _s._getframe(1)
        if fr.f_code.co_name in ("mm", "tr", "op"):
            fr = _s._getframe(2)
        self.src[eng].append(f"{fr.f_lineno}:{meth}")
        self.ops[eng].append(
            lambda e, m=meth, a=args, k2=dict(kw), sm=s:
            getattr(e, m)(*a, **k2).then_inc(self.sems[sm], 1))
        ev = (s, v, eng)
        for b in outs:
            b.w = ev
            b.r = []
            if eng == "vector":
                b.wvi = self.vec_idx
        for b in ins:
            b.r.append(ev)
        if eng == "vector":
            self.vec_idx += 1
        return ev

    def mm(self, outs, ins, *args, **kw):
        kw.setdefault("skip_group_check", True)
        return self.op("tensor", "matmul", outs, ins, *args, **kw)

    def tr(self, outs, ins, *args, **kw):
        return self.op("tensor", "transpose", outs, ins, *args, **kw)

    def dma(self, eng, sname, dst_ap, src_ap, outs, ins, **kw):
        self._waits(eng, ins, outs)
        v = self._bump(sname, 16)
        self.ops[eng].append(
            lambda e, o=dst_ap, i=src_ap, sm=sname, k2=dict(kw):
            e.dma_start(out=o, in_=i, **k2).then_inc(self.sems[sm], 16))
        ev = (sname, v, None)
        for b in outs:
            b.w = ev
            b.r = []
        for b in ins:
            b.r.append(ev)
        return ev

    def collective(self, wouts, wins, *args, **kw):
        self._waits("gpsimd", wins, wouts)
        v = self._bump("scc", 1)
        self.ops["gpsimd"].append(
            lambda e, a=args, k2=dict(kw):
            e.collective_compute(*a, **k2).then_inc(self.sems["scc"]))
        ev = ("scc", v, None)
        for b in wouts:
            b.w = ev
            b.r = []
        for b in wins:
            b.r.append(ev)
        return ev

    def raw(self, eng, fn):
        self.ops[eng].append(fn)


def build_nc():
    nc = bass.Bass("TRN2", target_bir_lowering=False,
                   detect_race_conditions=False)
    p = B()

    xin = nc.dram_tensor("x_in", [T, D], F32, kind="ExternalInput")
    hspec = [
        ("Wq", [D, D]), ("bq", [D, 1]), ("Wl", [D, LAT]), ("bl", [1, LAT]),
        ("Wk", [LAT, NKV * KD]), ("bk", [1, NKV * KD]),
        ("Wv", [LAT, NKV * KD]), ("bv", [1, NKV * KD]),
        ("Wq2kv", [HD, KD]), ("Wkv2h", [KD, HD]), ("bkv2h", [HD, 1]),
        ("Wo", [D, D]), ("bo", [1, D]),
        ("Wf", [D, D]), ("gam", [D, 1]),
        ("ident", [128, 128]), ("m0", [128, 128]),
    ]
    h = {name: nc.dram_tensor(name, shape, F32, kind="ExternalInput")
         for name, shape in hspec}
    yout = nc.dram_tensor("y", [T, D], F32, kind="ExternalOutput")
    dbg = nc.dram_tensor("dbg", [128, 1600], F32, kind="ExternalOutput") if DEBUG else None

    xw = nc.dram_tensor("x_work", [D, T], F32)
    wqTs = nc.dram_tensor("wqT_s", [D, D], BF16)
    wqe = {s: nc.dram_tensor(f"wqe_{s}", [D, NH * KD], BF16) for s in ("hi", "lo")}
    wls = {s: nc.dram_tensor(f"wl_{s}", [D, LAT], BF16) for s in ("hi", "lo")}
    wks = {s: nc.dram_tensor(f"wk_{s}", [LAT, NKV * KD], BF16) for s in ("hi", "lo")}
    wvs = {s: nc.dram_tensor(f"wv_{s}", [LAT, NKV * KD], BF16) for s in ("hi", "lo")}
    woe = {s: nc.dram_tensor(f"woe_{s}", [NH * KD, D], BF16) for s in ("hi", "lo")}
    cc_in = nc.dram_tensor("cc_in", [1, 8], F32)
    cc_out = nc.dram_tensor("cc_out", [1, 8], F32)
    scal_h = nc.dram_tensor("scal_h", [1, 8], F32)
    row_h = nc.dram_tensor("row_h", [1, D], F32)
    rows_h = nc.dram_tensor("rows_h", [2, T], F32)

    es = contextlib.ExitStack()
    with es:
        def sb(name, shape, dt=F32):
            t = es.enter_context(nc.sbuf_tensor(name, shape, dt))
            return t, Buf()

        def psm(name, shape, dt=F32):
            t = es.enter_context(nc.psum_tensor(name, shape, dt))
            return t, Buf()

        for s in ("sld", "sst", "smm", "sdv", "sac", "spl", "scc"):
            p.sems[s] = es.enter_context(nc.semaphore(s))

        xc, Bxc = sb("xc", [128, NF * C])
        abfA, BabfA = sb("abfA", [128, NF * C], BF16)
        abfB, BabfB = sb("abfB", [128, NF * C], BF16)
        qhi, Bqhi = sb("qhi", [128, NPAIR * C], BF16)
        qlo, Bqlo = sb("qlo", [128, NPAIR * C], BF16)
        lathi, Blathi = sb("lathi", [128, 2 * C], BF16)
        latlo, Blatlo = sb("latlo", [128, 2 * C], BF16)
        khi, Bkhi = sb("khi", [128, 2 * T], BF16)
        klo, Bklo = sb("klo", [128, 2 * T], BF16)
        vhi, Bvhi = sb("vhi", [128, (T // 128) * 256], BF16)
        vlo, Bvlo = sb("vlo", [128, (T // 128) * 256], BF16)
        af32, Baf32 = sb("af32", [128, 4 * C])
        ehi, Behi = zip(*[sb(f"ehi{i}", [128, C], BF16) for i in range(4)])
        elo, Belo = zip(*[sb(f"elo{i}", [128, C], BF16) for i in range(4)])
        wrow, Bwrow = sb("wrow", [128, D])
        wbf, Bwbf = sb("wbf", [128, D], BF16)
        wbf2, Bwbf2 = sb("wbf2", [128, D], BF16)
        wbhi, Bwbhi = zip(*[sb(f"wbhi{i}", [128, NF * 128], BF16) for i in range(2)])
        wblo, Bwblo = zip(*[sb(f"wblo{i}", [128, NF * 128], BF16) for i in range(2)])
        kvwh, Bkvwh = sb("kvwh", [128, 4 * 128], BF16)
        kvwl, Bkvwl = sb("kvwl", [128, 4 * 128], BF16)
        stage, Bstage = sb("stage", [128, C])
        t1, Bt1 = sb("t1", [128, C])
        t3, Bt3 = sb("t3", [128, C])
        scri, Bscri = sb("scri", [128, C], I32)
        scra, Bscra = sb("scra", [128, C])
        scrb, Bscrb = sb("scrb", [128, C])
        scrc, Bscrc = sb("scrc", [128, C])
        mb, Bmb = sb("mb", [128, C])
        rb, Brb = sb("rb", [128, C])
        rs0, Brs0 = sb("rs0", [1, C])
        rs1, Brs1 = sb("rs1", [1, C])
        m_r, Bm_r = sb("m_r", [1, C])
        r_r, Br_r = sb("r_r", [1, C])
        v_r, Bv_r = sb("v_r", [1, C])
        s_r, Bs_r = sb("s_r", [1, C])
        brow, Bbrow = sb("brow", [1, C])
        idt, Bidt = sb("idt", [128, 128])
        m0s, Bm0 = sb("m0s", [128, 128])
        onesc, Bonesc = sb("onesc", [128, 1])
        onescbf, Bonescbf = sb("onescbf", [128, 1], BF16)
        onesr, Bonesr = sb("onesr", [1, 128])
        ones33, Bones33 = sb("ones33", [128, 33])
        maskA, BmaskA = sb("maskA", [1, 128])
        maskB, BmaskB = sb("maskB", [1, 128])
        gcol, Bgcol = sb("gcol", [128, NF])
        q2kvh, Bq2kvh = sb("q2kvh", [128, KD], BF16)
        q2kvl, Bq2kvl = sb("q2kvl", [128, KD], BF16)
        kv2h_h, Bkv2h_h = sb("kv2h_h", [128, KD], BF16)
        kv2h_l, Bkv2h_l = sb("kv2h_l", [128, KD], BF16)
        bqcolh, Bbqcolh = sb("bqcolh", [128, 1], BF16)
        bqcoll, Bbqcoll = sb("bqcoll", [128, 1], BF16)
        bqc, Bbqc = sb("bqc", [128, 8])
        blc, Bblc = sb("blc", [128, 8])
        bkc, Bbkc = sb("bkc", [128, 2])
        bvc, Bbvc = sb("bvc", [128, 2])
        boc, Bboc = sb("boc", [128, NF])
        am, Bam = sb("am", [128, 1])
        s8, Bs8 = sb("s8", [1, 8])
        sc1, Bsc1 = sb("sc1", [1, 1])
        sc2, Bsc2 = sb("sc2", [1, 1])
        wsc, Bwsc = sb("wsc", [1, 1])
        rho, Brho = sb("rho", [128, 1])
        ahp, Bahp = sb("ahp", [128, 1])
        rwp, Brwp = sb("rwp", [128, 1])
        wscp, Bwscp = sb("wscp", [128, 1])
        mxc, Bmxc = sb("mxc", [128, 1])
        scp, Bscp = sb("scp", [128, 1])
        tmpc, Btmpc = sb("tmpc", [128, 1])
        sep, _Bsep = sb("sep", [128, 1])
        kmax2, Bkmax2 = sb("kmax2", [33, 2])
        qm2, Bqm2 = sb("qm2", [33, 8])
        bpair, Bbpair = sb("bpair", [33, 1])
        bnegT, BbnegT = sb("bnegT", [1, 33])
        biasA, BbiasA = sb("biasA", [128, 1])
        biasB, BbiasB = sb("biasB", [128, 1])

        rnd, Brnd = abfA, BabfA
        xnhi, Bxnhi = abfA, BabfA
        xnlo, Bxnlo = abfB, BabfB
        ctxhi = abfB[:, 0:NPAIR * C]
        ctxlo = abfB[:, NPAIR * C:2 * NPAIR * C]
        Bctxhi = Bctxlo = BabfB
        xtile, Bxtile = wrow, Bwrow
        wrow2 = af32
        Bwrow2 = Baf32
        ef = [af32[:, i * C:(i + 1) * C] for i in range(4)]
        Bef = [Baf32] * 4
        obf = wbhi[0][0:64, :]
        obf2 = wblo[0][0:64, :]
        Bobf, Bobf2 = Bwbhi[0], Bwblo[0]
        rbq, Brbq = mb, Bmb

        pp, Bpp = zip(*[psm(f"pp{i}", [128, C]) for i in range(2)])
        psA, BpsA = zip(*[psm(f"psA{i}", [128, C]) for i in range(2)])
        psB, BpsB = zip(*[psm(f"psB{i}", [128, C]) for i in range(2)])
        pctx, Bpctx = psm("pctx", [128, C])
        pms, Bpms = psm("pms", [128, C])

        Bscal = [Buf() for _ in range(8)]
        Bxw = Buf()
        BwqTs = Buf()
        Bwqe = {s: Buf() for s in ("hi", "lo")}
        Bwls = {s: Buf() for s in ("hi", "lo")}
        Bwks = {s: Buf() for s in ("hi", "lo")}
        Bwvs = {s: Buf() for s in ("hi", "lo")}
        Bwoe = {s: Buf() for s in ("hi", "lo")}
        Bccin, Bccout = Buf(), Buf()
        Bxin, By = Buf(), Buf()
        Browh = Buf()
        Browsh = Buf()
        Bh = {k: Buf() for k in h}

        p.sep_ap = sep[:]
        op = p.op
        mm = p.mm
        tr = p.tr
        dma = p.dma

        def split_hilo(src, Bsrc, dsthi, Bdh, dstlo, Bdl):
            op("vector", "tensor_copy", [Bdh], [Bsrc], dsthi, src)
            op("vector", "tensor_tensor", [Bdl], [Bsrc, Bdh], dstlo, src, dsthi,
               OP.subtract)

        def round_bf16(src, Bsrc, dst, Bdst, width):
            si = scri[:, 0:width]
            sa = scra[:, 0:width]
            sb_ = scrb[:, 0:width]
            op("vector", "tensor_copy", [Bscri], [Bsrc], si, src)
            op("vector", "scalar_tensor_tensor", [Bscra], [Bsrc, Bscri], sa, src,
               0.5, si, OP.add, OP.subtract)
            op("vector", "tensor_scalar", [Bscrb], [Bscra], sb_, sa, 1.0, None,
               OP.is_gt)
            op("vector", "scalar_tensor_tensor", [Bscra], [Bscra, Bscrb], sa, sa,
               0.0, sb_, OP.is_lt, OP.subtract)
            op("vector", "tensor_tensor", [Bdst], [Bscri, Bscra], dst, si, sa,
               OP.subtract)

        def tree(t, Bt, p_, col0, w, opk):
            while w > 1:
                w //= 2
                op("vector", "tensor_tensor", [Bt], [Bt],
                   t[0:p_, col0:col0 + w], t[0:p_, col0:col0 + w],
                   t[0:p_, col0 + w:col0 + 2 * w], opk)

        def bcast_scal(src_ap, Bsrc, dst, Bdst, lane):
            dma("gpsimd", "sst", scal_h[0:1, lane:lane + 1], src_ap,
                [Bscal[lane]], [Bsrc])
            dma("gpsimd", "sst", dst,
                scal_h[0:1, lane:lane + 1].broadcast_to((128, 1)),
                [Bdst], [Bscal[lane]])

        def cols_from_rowh(n, dstcols, Bdst):
            dma("gpsimd", "sst", dstcols,
                row_h[0:1, 0:n * 128].rearrange("a (f q) -> (a q) f", q=128),
                [Bdst], [Browh], allow_slow_non_contiguous=True)

        def bcast_row(src_ap, Bsrc, dst, Bdst):
            mm([Bpms], [Bsrc, Bonesr], pms[:], onesr[:], src_ap, start=True,
               stop=True)
            op("vector", "tensor_copy", [Bdst], [Bpms], dst, pms[:])

        op("vector", "memset", [], [], sep[:], 0.0)
        op("vector", "memset", [Bonesc], [], onesc[:], 1.0)
        op("vector", "memset", [Bonesr], [], onesr[:], 1.0)
        op("vector", "memset", [Bonescbf], [], onescbf[:], 1.0)
        op("vector", "memset", [Bones33], [], ones33[:], 0.0)
        op("vector", "memset", [Bones33], [], ones33[0:64, 0:1], 1.0)
        op("vector", "memset", [Bones33], [], ones33[64:128, 32:33], 1.0)
        op("vector", "memset", [BmaskA], [], maskA[:, 0:64], 1.0)
        op("vector", "memset", [BmaskA], [], maskA[:, 64:128], 0.0)
        op("vector", "memset", [BmaskB], [], maskB[:, 0:64], 0.0)
        op("vector", "memset", [BmaskB], [], maskB[:, 64:128], 1.0)
        dma("sync", "sld", idt[:], h["ident"].ap(), [Bidt], [])
        dma("sync", "sld", m0s[:], h["m0"].ap(), [Bm0], [])
        for f in range(NF):
            dma("sync", "sld", gcol[:, f:f + 1], h["gam"][f * 128:(f + 1) * 128, 0:1],
                [Bgcol], [])

        if "entry" in PHASES:
            for tt in range(T // 128):
                dma("sync", "sld", xtile[:], xin[tt * 128:(tt + 1) * 128, :],
                    [Bxtile], [Bxin])
                for g in range(4):
                    Bp = Bpp[g % 2]
                    for j in range(4):
                        f = g * 4 + j
                        tr([Bp], [Bxtile, Bidt], pp[g % 2][:, j * 128:(j + 1) * 128],
                           xtile[:, f * 128:(f + 1) * 128], idt[:])
                    op("vector", "tensor_copy", [Bstage], [Bp], stage[:], pp[g % 2][:])
                    for j in range(4):
                        f = g * 4 + j
                        dma("gpsimd", "sst",
                            xw[f * 128:(f + 1) * 128, tt * 128:(tt + 1) * 128],
                            stage[:, j * 128:(j + 1) * 128], [Bxw], [Bstage])

        if "prep" in PHASES:
            dma("sync", "sld", wrow[:, 0:KD], h["Wq2kv"].ap(), [Bwrow], [Bh["Wq2kv"]])
            split_hilo(wrow[:, 0:KD], Bwrow, q2kvh[:], Bq2kvh, q2kvl[:], Bq2kvl)
            dma("sync", "sld", wrow[0:KD, 128:256], h["Wkv2h"].ap(), [Bwrow],
                [Bh["Wkv2h"]])
            tr([Bpp[0]], [Bwrow, Bidt], pp[0][:, 0:KD], wrow[0:KD, 128:256],
               idt[0:KD, 0:KD])
            split_hilo(pp[0][:, 0:KD], Bpp[0], kv2h_h[:], Bkv2h_h, kv2h_l[:],
                       Bkv2h_l)

            accw = xc
            for o in range(NF):
                dma("sync", "sld", wrow[:], h["Wf"][o * 128:(o + 1) * 128, :],
                    [Bwrow], [Bh["Wf"]])
                op("scalar", "activation", [Bscrc], [Bwrow], scrc[:],
                   wrow[:, 0:C], AF.Abs)
                op("scalar", "activation", [Bscra], [Bwrow], scra[:],
                   wrow[:, C:2 * C], AF.Abs)
                op("scalar", "activation", [Bscrb], [Bwrow], scrb[:],
                   wrow[:, 2 * C:3 * C], AF.Abs)
                op("scalar", "activation", [Bt3], [Bwrow], t3[:],
                   wrow[:, 3 * C:4 * C], AF.Abs)
                for (buf, Bbuf, q4) in ((scrc, Bscrc, 0), (scra, Bscra, 1),
                                        (scrb, Bscrb, 2), (t3, Bt3, 3)):
                    if o == 0:
                        op("vector", "tensor_copy", [Bxc], [Bbuf],
                           accw[:, q4 * C:(q4 + 1) * C], buf[:])
                    else:
                        op("vector", "tensor_tensor", [Bxc], [Bxc, Bbuf],
                           accw[:, q4 * C:(q4 + 1) * C],
                           accw[:, q4 * C:(q4 + 1) * C], buf[:], OP.add)
            op("vector", "tensor_tensor", [Bxc], [Bxc], accw[:, 0:2 * C],
               accw[:, 0:2 * C], accw[:, 2 * C:4 * C], OP.add)
            tree(accw, Bxc, 128, 0, 2 * C, OP.add)
            dma("gpsimd", "sst", row_h[0:1, 0:128], accw[:, 0:1], [Browh], [Bxc])
            dma("gpsimd", "sst", scrc[:, 0:128],
                row_h[0:1, 0:128].broadcast_to((128, 128)), [Bscrc], [Browh])
            tree(scrc, Bscrc, 128, 0, 128, OP.add)
            op("vector", "tensor_scalar", [Bwscp], [Bscrc], wscp[:],
               scrc[:, 0:1], 1.0 / (D * D), 1e-5, OP.mult, OP.max)
            op("vector", "reciprocal", [Brwp], [Bwscp], rwp[:], wscp[:])
            if DEBUG:
                Bdbg = Buf()
                dma("gpsimd", "sst", dbg[:, 1570:1571], am[:], [Bdbg], [Bam],
                    allow_slow_non_contiguous=True)
                dma("gpsimd", "sst", dbg[:, 1571:1572], wscp[:], [Bdbg], [Bwscp],
                    allow_slow_non_contiguous=True)
                dma("gpsimd", "sst", dbg[:, 1573:1574], rwp[:], [Bdbg], [Brwp],
                    allow_slow_non_contiguous=True)

            for o in range(NF):
                dma("sync", "sld", wrow[:], h["Wf"][o * 128:(o + 1) * 128, :],
                    [Bwrow], [Bh["Wf"]])
                for q4 in range(4):
                    sl = slice(q4 * C, (q4 + 1) * C)
                    op("vector", "tensor_scalar", [Bwrow2], [Bwrow, Brwp],
                       wrow2[:, 0:C], wrow[:, sl], rwp[:], None, OP.mult)
                    op("vector", "tensor_scalar", [Bwrow2], [Bwrow2], wrow2[:, 0:C],
                       wrow2[:, 0:C], -1.0, 1.0, OP.max, OP.min)
                    round_bf16(wrow2[:, 0:C], Bwrow2, wbf[:, sl], Bwbf, C)
                for g in range(4):
                    Bp = Bpp[g % 2]
                    for j in range(4):
                        f = g * 4 + j
                        op("vector", "tensor_copy", [Bscrc], [Bwbf],
                           scrc[:, j * 128:(j + 1) * 128],
                           wbf[:, f * 128:(f + 1) * 128])
                    for j in range(4):
                        tr([Bp], [Bscrc, Bidt], pp[g % 2][:, j * 128:(j + 1) * 128],
                           scrc[:, j * 128:(j + 1) * 128], idt[:])
                    op("vector", "tensor_copy", [Bwbf2], [Bp],
                       wbf2[:, g * C:(g + 1) * C], pp[g % 2][:])
                for f in range(NF):
                    dma("gpsimd", "sst",
                        wqTs[f * 128:(f + 1) * 128, o * 128:(o + 1) * 128],
                        wbf2[:, f * 128:(f + 1) * 128], [BwqTs], [Bwbf2])

            for j in range(NPAIR):
                for ab in range(2):
                    hh = PAIR_A[j] + 4 * ab
                    dst = j * 128 + ab * 64
                    dma("sync", "sld",
                        wrow[:].rearrange("p (c j) -> p c j", j=128),
                        h["Wq"][:, hh * 128:(hh + 1) * 128].rearrange(
                            "(c pq) jj -> pq c jj", pq=128),
                        [Bwrow], [Bh["Wq"]])
                    for g in range(4):
                        Bp = Bpp[g % 2]
                        for jj in range(4):
                            f = g * 4 + jj
                            tr([Bp], [Bwrow, Bidt],
                               pp[g % 2][:, jj * 128:(jj + 1) * 128],
                               wrow[:, f * 128:(f + 1) * 128], idt[:])
                        split_hilo(pp[g % 2][:], Bp, wbf[:, g * C:(g + 1) * C],
                                   Bwbf, wbf2[:, g * C:(g + 1) * C], Bwbf2)
                    for half in range(2):
                        Bp = Bpp[half]
                        for cc in range(8):
                            c = half * 8 + cc
                            bsl = slice(c * 128, (c + 1) * 128)
                            osl = slice(cc * KD, (cc + 1) * KD)
                            mm([Bp], [Bwbf, Bq2kvh], pp[half][:, osl],
                               wbf[:, bsl], q2kvh[:], start=True, stop=False)
                            mm([Bp], [Bwbf, Bq2kvl], pp[half][:, osl],
                               wbf[:, bsl], q2kvl[:], start=False, stop=False)
                            mm([Bp], [Bwbf2, Bq2kvh], pp[half][:, osl],
                               wbf2[:, bsl], q2kvh[:], start=False, stop=True)
                        split_hilo(pp[half][:], Bp, wbf[:, half * C:(half + 1) * C],
                                   Bwbf, wbf2[:, half * C:(half + 1) * C], Bwbf2)
                        for cc in range(8):
                            c = half * 8 + cc
                            dma("gpsimd", "sst",
                                wqe["hi"][c * 128:(c + 1) * 128, dst:dst + KD],
                                wbf[:, half * C + cc * KD: half * C + (cc + 1) * KD],
                                [Bwqe["hi"]], [Bwbf])
                            dma("gpsimd", "sst",
                                wqe["lo"][c * 128:(c + 1) * 128, dst:dst + KD],
                                wbf2[:, half * C + cc * KD: half * C + (cc + 1) * KD],
                                [Bwqe["lo"]], [Bwbf2])

            dma("sync", "sld", wrow[0:1, 0:128] if False else wrow[0:128, 0:1],
                h["bkv2h"].ap(), [Bwrow], [Bh["bkv2h"]])
            op("vector", "tensor_copy", [Bscrc], [Bwrow], scrc[:, 0:1],
               wrow[0:128, 0:1])
            nh_i = 0
            for j in range(NPAIR):
                for ab in range(2):
                    hh = PAIR_A[j] + 4 * ab
                    dst = j * 128 + ab * 64
                    dma("sync", "sld", wrow[:], h["Wo"][hh * 128:(hh + 1) * 128, :],
                        [Bwrow], [Bh["Wo"]])
                    split_hilo(wrow[:], Bwrow, wbf[:], Bwbf, wbf2[:], Bwbf2)
                    for cc in range(4):
                        sl = slice(cc * C, (cc + 1) * C)
                        Bp = Bpp[cc % 2]
                        mm([Bp], [Bkv2h_h, Bwbf], pp[cc % 2][0:KD, :],
                           kv2h_h[:], wbf[:, sl], start=True, stop=False)
                        mm([Bp], [Bkv2h_l, Bwbf], pp[cc % 2][0:KD, :],
                           kv2h_l[:], wbf[:, sl], start=False, stop=False)
                        mm([Bp], [Bkv2h_h, Bwbf2], pp[cc % 2][0:KD, :],
                           kv2h_h[:], wbf2[:, sl], start=False, stop=True)
                        mm([Bpms], [Bscrc, Bwrow], pms[32 * cc:32 * cc + 1, :],
                           scrc[:, 0:1], wrow[:, sl],
                           start=(nh_i == 0), stop=(nh_i == NH - 1),
                           tile_position=(0, 32 * cc))
                        split_hilo(pp[cc % 2][0:KD, :], Bp, obf[:, sl], Bobf,
                                   obf2[:, sl], Bobf2)
                    dma("gpsimd", "sst", woe["hi"][dst:dst + KD, :], obf[:],
                        [Bwoe["hi"]], [Bobf])
                    dma("gpsimd", "sst", woe["lo"][dst:dst + KD, :], obf2[:],
                        [Bwoe["lo"]], [Bobf2])
                    nh_i += 1
            for cc in range(4):
                dma("sync", "sld", brow[:], h["bo"][0:1, cc * C:(cc + 1) * C],
                    [Bbrow], [Bh["bo"]])
                op("vector", "tensor_tensor", [Bbrow], [Bpms, Bbrow], brow[:],
                   pms[32 * cc:32 * cc + 1, :], brow[:], OP.add)
                dma("gpsimd", "sst", row_h[0:1, cc * C:(cc + 1) * C], brow[:],
                    [Browh], [Bbrow])
            cols_from_rowh(NF, boc[:], Bboc)

            for j in range(NPAIR):
                for ab in range(2):
                    hh = PAIR_A[j] + 4 * ab
                    dma("sync", "sld", wrow[0:128, 0:1],
                        h["bq"][hh * 128:(hh + 1) * 128, 0:1], [Bwrow], [Bh["bq"]])
                    split_hilo(wrow[0:128, 0:1], Bwrow, bqcolh[:], Bbqcolh,
                               bqcoll[:], Bbqcoll)
                    mm([Bpms], [Bbqcolh, Bq2kvh], pms[0:1, 0:KD], bqcolh[:],
                       q2kvh[:], start=True, stop=False)
                    mm([Bpms], [Bbqcolh, Bq2kvl], pms[0:1, 0:KD], bqcolh[:],
                       q2kvl[:], start=False, stop=False)
                    mm([Bpms], [Bbqcoll, Bq2kvh], pms[0:1, 0:KD], bqcoll[:],
                       q2kvh[:], start=False, stop=True)
                    op("vector", "tensor_copy", [Bbrow], [Bpms], brow[0:1, 0:KD],
                       pms[0:1, 0:KD])
                    dma("gpsimd", "sst",
                        row_h[0:1, j * 128 + ab * 64: j * 128 + ab * 64 + KD],
                        brow[0:1, 0:KD], [Browh], [Bbrow])
            cols_from_rowh(8, bqc[:], Bbqc)

            for (src, Bsrc_, n, dstc, Bdstc) in (
                    (h["bl"], Bh["bl"], 8, blc, Bblc),
                    (h["bk"], Bh["bk"], 2, bkc, Bbkc),
                    (h["bv"], Bh["bv"], 2, bvc, Bbvc)):
                dma("gpsimd", "sst", row_h[0:1, 0:n * 128], src[0:1, 0:n * 128],
                    [Browh], [Bsrc_])
                cols_from_rowh(n, dstc[:], Bdstc)

            for o in range(NF):
                dma("sync", "sld", wrow[:, 0:LAT], h["Wl"][o * 128:(o + 1) * 128, :],
                    [Bwrow], [Bh["Wl"]])
                split_hilo(wrow[:, 0:LAT], Bwrow, wbf[:, 0:LAT], Bwbf,
                           wbf2[:, 0:LAT], Bwbf2)
                dma("gpsimd", "sst", wls["hi"][o * 128:(o + 1) * 128, :],
                    wbf[:, 0:LAT], [Bwls["hi"]], [Bwbf])
                dma("gpsimd", "sst", wls["lo"][o * 128:(o + 1) * 128, :],
                    wbf2[:, 0:LAT], [Bwls["lo"]], [Bwbf2])
            for (mat, Bmat, dsts, Bd) in ((h["Wk"], Bh["Wk"], wks, Bwks),
                                          (h["Wv"], Bh["Wv"], wvs, Bwvs)):
                for o in range(8):
                    dma("sync", "sld", wrow[:, 0:256], mat[o * 128:(o + 1) * 128, :],
                        [Bwrow], [Bmat])
                    split_hilo(wrow[:, 0:256], Bwrow, wbf[:, 0:256], Bwbf,
                               wbf2[:, 0:256], Bwbf2)
                    dma("gpsimd", "sst", dsts["hi"][o * 128:(o + 1) * 128, :],
                        wbf[:, 0:256], [Bd["hi"]], [Bwbf])
                    dma("gpsimd", "sst", dsts["lo"][o * 128:(o + 1) * 128, :],
                        wbf2[:, 0:256], [Bd["lo"]], [Bwbf2])

        def ln_chunk_stats(extra_rms):
            for f in range(NF):
                op("scalar", "activation", [Bscra], [Bxc], scra[:],
                   xc[:, f * C:(f + 1) * C], AF.Square)
                mm([Bpms], [Bxc, Bonesc], pms[0:1, :], onesc[:],
                   xc[:, f * C:(f + 1) * C], start=(f == 0), stop=(f == NF - 1))
                mm([Bpctx], [Bscra, Bonesc], pctx[0:1, :], onesc[:], scra[:],
                   start=(f == 0), stop=(f == NF - 1))
            op("vector", "tensor_scalar", [Bm_r], [Bpms], m_r[:], pms[0:1, :],
               1.0 / D, None, OP.mult)
            op("vector", "tensor_scalar", [Bv_r], [Bpctx], v_r[:], pctx[0:1, :],
               1.0 / D, None, OP.mult)
            op("vector", "tensor_tensor", [Bs_r], [Bm_r], s_r[:], m_r[:], m_r[:],
               OP.mult)
            op("vector", "tensor_tensor", [Bv_r], [Bv_r, Bs_r], v_r[:], v_r[:],
               s_r[:], OP.subtract)
            if extra_rms:
                op("vector", "tensor_scalar", [Bv_r], [Bv_r], v_r[:], v_r[:],
                   1.0 + 1e-6, 1e-11, OP.mult, OP.add)
            else:
                op("vector", "tensor_scalar", [Bv_r], [Bv_r], v_r[:], v_r[:],
                   1e-5, None, OP.add)
            op("vector", "reciprocal", [Bs_r], [Bv_r], s_r[:], v_r[:])
            op("scalar", "activation", [Br_r], [Bs_r], r_r[:], s_r[:], AF.Sqrt)

        ffn_calls = [0]

        def ffn_dbg(c, o):
            if not DEBUG or ffn_calls[0] != 1 or c != 0 or o != 0:
                return
            Bdbg = Buf()
            op("vector", "tensor_copy", [Bscrb], [Brnd], scrb[:], rnd[:, 0:C])
            dma("gpsimd", "sst", dbg[:, 0:C], scrb[:], [Bdbg], [Bscrb])
            op("vector", "tensor_copy", [Bscrb], [Bwbhi[0]], scrb[:],
               wbhi[0][:, 0:C])
            dma("gpsimd", "sst", dbg[:, C:2 * C], scrb[:], [Bdbg], [Bscrb])
            dma("gpsimd", "sst", dbg[:, 2 * C:3 * C], xc[:, 0:C], [Bdbg], [Bxc])
            dma("gpsimd", "sst", dbg[:, 1536:1537], rho[:], [Bdbg], [Brho], allow_slow_non_contiguous=True)
            dma("gpsimd", "sst", dbg[:, 1537:1538], ahp[:], [Bdbg], [Bahp], allow_slow_non_contiguous=True)
            dma("gpsimd", "sst", dbg[:, 1538:1539], rwp[:], [Bdbg], [Brwp], allow_slow_non_contiguous=True)
            dma("gpsimd", "sst", dbg[:, 1539:1540], am[:], [Bdbg], [Bam], allow_slow_non_contiguous=True)
            dma("gpsimd", "sst", dbg[:, 1540:1541], wscp[:], [Bdbg], [Bwscp],
                allow_slow_non_contiguous=True)

        def ffn(final):
            ffn_calls[0] += 1
            axmx = af32[:, 0:C]
            axmn = af32[:, C:2 * C]
            op("vector", "memset", [Baf32], [], axmx[:, :], -3.0e38)
            op("vector", "memset", [Baf32], [], axmn[:, :], 3.0e38)
            for c in range(NCH):
                sl = slice(c * C, (c + 1) * C)
                for f in range(NF):
                    dma("sync", "sld", xc[:, f * C:(f + 1) * C],
                        xw[f * 128:(f + 1) * 128, sl], [Bxc], [Bxw])
                ln_chunk_stats(extra_rms=True)
                dma("gpsimd", "sst", rows_h[0:1, sl], m_r[:], [Browsh], [Bm_r])
                dma("gpsimd", "sst", rows_h[1:2, sl], r_r[:], [Browsh], [Br_r])
                bcast_row(m_r[:], Bm_r, mb[:], Bmb)
                bcast_row(r_r[:], Br_r, rb[:], Brb)
                for f in range(NF):
                    op("vector", "tensor_tensor", [Bt1], [Bxc, Bmb], t1[:],
                       xc[:, f * C:(f + 1) * C], mb[:], OP.subtract)
                    op("vector", "scalar_tensor_tensor", [Bt3], [Bt1, Bgcol, Brb],
                       t3[:], t1[:], gcol[:, f:f + 1], rb[:], OP.mult, OP.mult)
                    op("vector", "tensor_tensor", [Baf32], [Baf32, Bt3],
                       axmx[:, :], axmx[:, :], t3[:], OP.max)
                    op("vector", "tensor_tensor", [Baf32], [Baf32, Bt3],
                       axmn[:, :], axmn[:, :], t3[:], OP.min)
            tree(af32, Baf32, 128, 0, C, OP.max)
            tree(af32, Baf32, 128, C, C, OP.min)
            op("vector", "tensor_scalar", [Btmpc], [Baf32], tmpc[:],
               af32[:, C:C + 1], -1.0, None, OP.mult)
            op("vector", "tensor_tensor", [Bmxc], [Baf32, Btmpc], mxc[:],
               af32[:, 0:1], tmpc[:], OP.max)
            dma("gpsimd", "sst", row_h[0:1, 0:128], mxc[:, 0:1], [Browh], [Bmxc])
            dma("gpsimd", "sst", scrc[:, 0:128],
                row_h[0:1, 0:128].broadcast_to((128, 128)), [Bscrc], [Browh])
            tree(scrc, Bscrc, 128, 0, 128, OP.max)
            op("vector", "tensor_copy", [Bmxc], [Bscrc], mxc[:], scrc[:, 0:1])
            dma("gpsimd", "sst", cc_in[0:1, :], mxc[0:8, 0:1], [Bccin], [Bmxc])
            p.collective([Bccout], [Bccin], "AllReduce", OP.max,
                         replica_groups=[list(range(8))],
                         ins=[cc_in.ap().opt()], outs=[cc_out.ap().opt()])
            dma("gpsimd", "sst", scp[:],
                cc_out[0:1, 0:1].broadcast_to((128, 1)), [Bscp], [Bccout])
            op("vector", "tensor_scalar", [Btmpc], [Bscp], tmpc[:], scp[:],
               1e-5, None, OP.max)
            op("vector", "tensor_copy", [Bscp], [Btmpc], scp[:], tmpc[:])
            op("vector", "reciprocal", [Btmpc], [Bscp], tmpc[:], scp[:])
            op("vector", "tensor_scalar", [Brho], [Btmpc], rho[:], tmpc[:],
               127.0, None, OP.mult)
            op("vector", "tensor_scalar", [Btmpc], [Bscp], tmpc[:], scp[:],
               0.5 / 127.0, None, OP.mult)
            op("vector", "tensor_tensor", [Bahp], [Btmpc, Bwscp], ahp[:],
               tmpc[:], wscp[:], OP.mult)
            if DEBUG and ffn_calls[0] == 1:
                Bdbg = Buf()
                dma("gpsimd", "sst", dbg[:, 1543:1544], scp[:], [Bdbg], [Bscp],
                    allow_slow_non_contiguous=True)
                dma("gpsimd", "sst", dbg[:, 1544:1545], rho[:], [Bdbg], [Brho],
                    allow_slow_non_contiguous=True)

            for c in range(NCH):
                sl = slice(c * C, (c + 1) * C)
                for f in range(NF):
                    dma("sync", "sld", xc[:, f * C:(f + 1) * C],
                        xw[f * 128:(f + 1) * 128, sl], [Bxc], [Bxw])
                dma("sync", "sld", m_r[:], rows_h[0:1, sl], [Bm_r], [Browsh])
                dma("sync", "sld", r_r[:], rows_h[1:2, sl], [Br_r], [Browsh])
                bcast_row(m_r[:], Bm_r, mb[:], Bmb)
                bcast_row(r_r[:], Br_r, rb[:], Brb)
                op("vector", "tensor_scalar", [Brb], [Brb, Brho], rb[:], rb[:],
                   rho[:], None, OP.mult)
                for f in range(NF):
                    op("vector", "tensor_tensor", [Bt1], [Bxc, Bmb], t1[:],
                       xc[:, f * C:(f + 1) * C], mb[:], OP.subtract)
                    op("vector", "scalar_tensor_tensor", [Bt3], [Bt1, Bgcol, Brb],
                       t3[:], t1[:], gcol[:, f:f + 1], rb[:], OP.mult, OP.mult)
                    op("vector", "tensor_scalar", [Bt3], [Bt3], t3[:], t3[:],
                       -128.0, 127.0, OP.max, OP.min)
                    round_bf16(t3[:], Bt3, rnd[:, f * C:(f + 1) * C], Brnd, C)
                for o in range(NF):
                    par = o % 2
                    dma("sync", "sld",
                        wbhi[par][:].rearrange("p (c j) -> p c j", j=128),
                        wqTs[:, o * 128:(o + 1) * 128].rearrange(
                            "(kc pq) jj -> pq kc jj", pq=128),
                        [Bwbhi[par]], [BwqTs])
                    for kc in range(NF):
                        mm([Bpp[par]], [Bwbhi[par], Brnd], pp[par][:],
                           wbhi[par][:, kc * 128:(kc + 1) * 128],
                           rnd[:, kc * C:(kc + 1) * C],
                           start=(kc == 0), stop=(kc == NF - 1))
                    op("vector", "scalar_tensor_tensor", [Bxc],
                       [Bpp[par], Bahp, Bxc], xc[:, o * C:(o + 1) * C],
                       pp[par][:], ahp[:], xc[:, o * C:(o + 1) * C],
                       OP.mult, OP.add)
                    ffn_dbg(c, o)
                if not final:
                    for f in range(NF):
                        dma("gpsimd", "sst", xw[f * 128:(f + 1) * 128, sl],
                            xc[:, f * C:(f + 1) * C], [Bxw], [Bxc])
                else:
                    for f in range(NF):
                        for ts_ in range(4):
                            par2 = ts_ % 2
                            tr([Bpp[par2]], [Bxc, Bidt], pp[par2][:, 0:128],
                               xc[:, f * C + ts_ * 128: f * C + (ts_ + 1) * 128],
                               idt[:])
                            op("vector", "tensor_copy", [Bstage], [Bpp[par2]],
                               stage[:, ts_ * 128:(ts_ + 1) * 128],
                               pp[par2][:, 0:128])
                        for ts_ in range(4):
                            dma("gpsimd", "sst",
                                yout[c * C + ts_ * 128: c * C + (ts_ + 1) * 128,
                                     f * 128:(f + 1) * 128],
                                stage[:, ts_ * 128:(ts_ + 1) * 128], [By],
                                [Bstage])

        if "ffn1" in PHASES:
            ffn(final=False)

        if "attn" in PHASES:
            op("vector", "memset", [Bkmax2], [], kmax2[:], 0.0)
            for qc in range(ATTN_NCH):
                sl = slice(qc * C, (qc + 1) * C)
                for f in range(NF):
                    dma("sync", "sld", xc[:, f * C:(f + 1) * C],
                        xw[f * 128:(f + 1) * 128, sl], [Bxc], [Bxw])
                ln_chunk_stats(extra_rms=False)
                bcast_row(m_r[:], Bm_r, mb[:], Bmb)
                bcast_row(r_r[:], Br_r, rb[:], Brb)
                for f in range(NF):
                    op("vector", "tensor_tensor", [Bt1], [Bxc, Bmb], t1[:],
                       xc[:, f * C:(f + 1) * C], mb[:], OP.subtract)
                    op("vector", "tensor_tensor", [Bt3], [Bt1, Brb], t3[:],
                       t1[:], rb[:], OP.mult)
                    split_hilo(t3[:], Bt3, xnhi[:, f * C:(f + 1) * C], Bxnhi,
                               xnlo[:, f * C:(f + 1) * C], Bxnlo)
                for o in range(NPAIR):
                    par = o % 2
                    for (wsrc, Bwd, wb_, Bwb_) in (
                            (wqe["hi"], Bwqe["hi"], wbhi[par], Bwbhi[par]),
                            (wqe["lo"], Bwqe["lo"], wblo[par], Bwblo[par])):
                        dma("sync", "sld",
                            wb_[:].rearrange("p (c j) -> p c j", j=128),
                            wsrc[:, o * 128:(o + 1) * 128].rearrange(
                                "(kc pq) jj -> pq kc jj", pq=128),
                            [Bwb_], [Bwd])
                    for kc in range(NF):
                        ksl = slice(kc * 128, (kc + 1) * 128)
                        xsl = slice(kc * C, (kc + 1) * C)
                        mm([Bpp[par]], [Bwbhi[par], Bxnhi], pp[par][:],
                           wbhi[par][:, ksl], xnhi[:, xsl],
                           start=(kc == 0), stop=False)
                        mm([Bpp[par]], [Bwbhi[par], Bxnlo], pp[par][:],
                           wbhi[par][:, ksl], xnlo[:, xsl], start=False,
                           stop=False)
                        mm([Bpp[par]], [Bwblo[par], Bxnhi], pp[par][:],
                           wblo[par][:, ksl], xnhi[:, xsl], start=False,
                           stop=(kc == NF - 1))
                    op("vector", "tensor_scalar", [Bscra], [Bpp[par], Bbqc],
                       scra[:], pp[par][:], bqc[:, o:o + 1], None, OP.add)
                    split_hilo(scra[:], Bscra, qhi[:, o * C:(o + 1) * C], Bqhi,
                               qlo[:, o * C:(o + 1) * C], Bqlo)
                    op("vector", "tensor_tensor", [Bscrb], [Bscra], scrb[:],
                       scra[:], scra[:], OP.mult)
                    mm([Bpms], [Bscrb, Bones33], pms[0:33, :], ones33[:],
                       scrb[:], start=True, stop=True)
                    op("vector", "tensor_copy", [Bscrb], [Bpms], scrb[0:33, :],
                       pms[0:33, :])
                    tree(scrb, Bscrb, 33, 0, C, OP.max)
                    op("vector", "tensor_copy", [Bqm2], [Bscrb],
                       qm2[:, o:o + 1], scrb[0:33, 0:1])
                for o in range(8):
                    par = o % 2
                    for (wsrc, Bwd, wb_, Bwb_) in (
                            (wls["hi"], Bwls["hi"], wbhi[par], Bwbhi[par]),
                            (wls["lo"], Bwls["lo"], wblo[par], Bwblo[par])):
                        dma("sync", "sld",
                            wb_[:].rearrange("p (c j) -> p c j", j=128),
                            wsrc[:, o * 128:(o + 1) * 128].rearrange(
                                "(kc pq) jj -> pq kc jj", pq=128),
                            [Bwb_], [Bwd])
                    for kc in range(NF):
                        ksl = slice(kc * 128, (kc + 1) * 128)
                        xsl = slice(kc * C, (kc + 1) * C)
                        mm([Bpp[par]], [Bwbhi[par], Bxnhi], pp[par][:],
                           wbhi[par][:, ksl], xnhi[:, xsl],
                           start=(kc == 0), stop=False)
                        mm([Bpp[par]], [Bwbhi[par], Bxnlo], pp[par][:],
                           wbhi[par][:, ksl], xnlo[:, xsl], start=False,
                           stop=False)
                        mm([Bpp[par]], [Bwblo[par], Bxnhi], pp[par][:],
                           wblo[par][:, ksl], xnhi[:, xsl], start=False,
                           stop=(kc == NF - 1))
                    op("scalar", "activation", [Bscra], [Bpp[par], Bblc],
                       scra[:], pp[par][:], AF.Silu, bias=blc[:, o:o + 1],
                       scale=1.0)
                    lsl = slice(par * C, (par + 1) * C)
                    split_hilo(scra[:], Bscra, lathi[:, lsl], Blathi,
                               latlo[:, lsl], Blatlo)
                    dma("sync", "sld", kvwh[:, 0:256],
                        wks["hi"][o * 128:(o + 1) * 128, :], [Bkvwh], [Bwks["hi"]])
                    dma("sync", "sld", kvwh[:, 256:512],
                        wvs["hi"][o * 128:(o + 1) * 128, :], [Bkvwh], [Bwvs["hi"]])
                    dma("sync", "sld", kvwl[:, 0:256],
                        wks["lo"][o * 128:(o + 1) * 128, :], [Bkvwl], [Bwks["lo"]])
                    dma("sync", "sld", kvwl[:, 256:512],
                        wvs["lo"][o * 128:(o + 1) * 128, :], [Bkvwl], [Bwvs["lo"]])
                    for tgt in range(4):
                        pt = (psA[0], psA[1], psB[0], psB[1])[tgt]
                        Bpt = (BpsA[0], BpsA[1], BpsB[0], BpsB[1])[tgt]
                        wsl = slice(tgt * 128, (tgt + 1) * 128)
                        mm([Bpt], [Bkvwh, Blathi], pt[:], kvwh[:, wsl],
                           lathi[:, lsl], start=(o == 0), stop=False)
                        mm([Bpt], [Bkvwh, Blatlo], pt[:], kvwh[:, wsl],
                           latlo[:, lsl], start=False, stop=False)
                        mm([Bpt], [Bkvwl, Blathi], pt[:], kvwl[:, wsl],
                           lathi[:, lsl], start=False, stop=(o == 7))
                for o2 in range(2):
                    op("vector", "tensor_scalar", [Bscra], [BpsA[o2], Bbkc],
                       scra[:], psA[o2][:], bkc[:, o2:o2 + 1], None, OP.add)
                    split_hilo(scra[:], Bscra,
                               khi[:, o2 * T + qc * C: o2 * T + (qc + 1) * C],
                               Bkhi,
                               klo[:, o2 * T + qc * C: o2 * T + (qc + 1) * C],
                               Bklo)
                    op("vector", "tensor_tensor", [Bscrb], [Bscra], scrb[:],
                       scra[:], scra[:], OP.mult)
                    mm([Bpms], [Bscrb, Bones33], pms[0:33, :], ones33[:],
                       scrb[:], start=True, stop=True)
                    op("vector", "tensor_copy", [Bscrb], [Bpms], scrb[0:33, :],
                       pms[0:33, :])
                    tree(scrb, Bscrb, 33, 0, C, OP.max)
                    op("vector", "tensor_tensor", [Bkmax2], [Bkmax2, Bscrb],
                       kmax2[:, o2:o2 + 1], kmax2[:, o2:o2 + 1],
                       scrb[0:33, 0:1], OP.max)
                for o2 in range(2):
                    op("vector", "tensor_scalar", [Bscra], [BpsB[o2], Bbvc],
                       scra[:], psB[o2][:], bvc[:, o2:o2 + 1], None, OP.add)
                    split_hilo(scra[:], Bscra, wbf[:, 0:C], Bwbf,
                               wbf2[:, 0:C], Bwbf2)
                    for ts_ in range(4):
                        tile = qc * 4 + ts_
                        for hl in range(2):
                            src = wbf if hl == 0 else wbf2
                            Bsrc = Bwbf if hl == 0 else Bwbf2
                            dst = vhi if hl == 0 else vlo
                            Bdst = Bvhi if hl == 0 else Bvlo
                            op("vector", "tensor_copy", [Bscrb], [Bsrc],
                               scrb[:, 0:128], src[:, ts_ * 128:(ts_ + 1) * 128])
                            tr([Bpp[0]], [Bscrb, Bidt], pp[0][:, 0:128],
                               scrb[:, 0:128], idt[:])
                            op("vector", "tensor_copy", [Bdst], [Bpp[0]],
                               dst[:, tile * 256 + o2 * 128:
                                   tile * 256 + (o2 + 1) * 128],
                               pp[0][:, 0:128])
                for j in range(ATTN_NPAIR if ATTN_INNER else 0):
                    kch = j // 4
                    ga = 2 * kch
                    op("vector", "tensor_tensor", [Bbpair], [Bqm2, Bkmax2],
                       bpair[:], qm2[:, j:j + 1], kmax2[:, kch:kch + 1], OP.mult)
                    op("scalar", "activation", [Bbpair], [Bbpair], bpair[:],
                       bpair[:], AF.Sqrt)
                    op("vector", "tensor_scalar", [Bbpair], [Bbpair], bpair[:],
                       bpair[:], -0.125, None, OP.mult)
                    tr([BpsB[0]], [Bbpair, Bidt], psB[0][0:1, 0:33],
                       bpair[0:33, :], idt[0:33, 0:33])
                    op("vector", "tensor_copy", [BbnegT], [BpsB[0]], bnegT[:],
                       psB[0][0:1, 0:33])
                    mm([BpsB[0]], [BbnegT, Bonesr], psB[0][:, 0:1], onesr[:],
                       bnegT[0:1, 0:1], start=True, stop=True)
                    op("vector", "tensor_copy", [BbiasA], [BpsB[0]], biasA[:],
                       psB[0][:, 0:1])
                    mm([BpsB[0]], [BbnegT, Bonesr], psB[0][:, 0:1], onesr[:],
                       bnegT[0:1, 32:33], start=True, stop=True)
                    op("vector", "tensor_copy", [BbiasB], [BpsB[0]], biasB[:],
                       psB[0][:, 0:1])
                    nkt = 4 * qc + 4
                    for kt in range(nkt if ATTN_LVL >= 2 else 0):
                        par = kt % 2
                        off = 0 if kt < 4 * qc else (kt - 4 * qc) * 128
                        w_ = C - off
                        ktsl = slice(kch * T + kt * 128, kch * T + (kt + 1) * 128)
                        qsl = slice(j * C + off, (j + 1) * C)
                        esl = slice(0, w_)
                        mm([BpsA[par]], [Bkhi, Bqhi], psA[par][:, esl],
                           khi[0:64, ktsl], qhi[0:64, qsl], start=True, stop=False)
                        mm([BpsA[par]], [Bkhi, Bqlo], psA[par][:, esl],
                           khi[0:64, ktsl], qlo[0:64, qsl], start=False,
                           stop=False)
                        mm([BpsA[par]], [Bklo, Bqhi], psA[par][:, esl],
                           klo[0:64, ktsl], qhi[0:64, qsl], start=False,
                           stop=True)
                        mm([BpsB[par]], [Bkhi, Bqhi], psB[par][:, esl],
                           khi[64:128, ktsl], qhi[64:128, qsl], start=True,
                           stop=False)
                        mm([BpsB[par]], [Bkhi, Bqlo], psB[par][:, esl],
                           khi[64:128, ktsl], qlo[64:128, qsl], start=False,
                           stop=False)
                        mm([BpsB[par]], [Bklo, Bqhi], psB[par][:, esl],
                           klo[64:128, ktsl], qhi[64:128, qsl], start=False,
                           stop=True)
                        for hd2 in range(2 if ATTN_LVL >= 3 else 0):
                            pse = psA[par] if hd2 == 0 else psB[par]
                            Bpse = BpsA[par] if hd2 == 0 else BpsB[par]
                            bias_ = biasA if hd2 == 0 else biasB
                            Bbias_ = BbiasA if hd2 == 0 else BbiasB
                            ei = par * 2 + hd2
                            op("scalar", "activation", [Bef[ei]], [Bpse, Bbias_],
                               ef[ei][:, esl], pse[:, esl], AF.Exp,
                               bias=bias_[:], scale=0.125)
                            if kt >= 4 * qc:
                                op("vector", "tensor_tensor", [Bef[ei]],
                                   [Bef[ei], Bm0], ef[ei][:, 0:128],
                                   ef[ei][:, 0:128], m0s[:], OP.mult)
                            op("vector", "tensor_copy", [Behi[ei]], [Bef[ei]],
                               ehi[ei][:, esl], ef[ei][:, esl])
                            op("vector", "tensor_tensor", [Belo[ei]],
                               [Bef[ei], Behi[ei]], elo[ei][:, esl],
                               ef[ei][:, esl], ehi[ei][:, esl], OP.subtract)
                            if ATTN_LVL < 4:
                                continue
                            prs = pms if hd2 == 0 else pp[0]
                            Bprs = Bpms if hd2 == 0 else Bpp[0]
                            mm([Bprs], [Behi[ei], Bonescbf],
                               prs[0:1, off:C], onescbf[:],
                               ehi[ei][:, esl], start=(kt == 0), stop=False)
                            mm([Bprs], [Belo[ei], Bonescbf],
                               prs[0:1, off:C], onescbf[:],
                               elo[ei][:, esl], start=False,
                               stop=(kt == nkt - 1))
                            if ATTN_LVL < 5:
                                continue
                            g = ga + hd2
                            vt = slice(kt * 256 + g * 64, kt * 256 + (g + 1) * 64)
                            ob = hd2 * 64
                            tp = (0, ob)
                            mm([Bpctx], [Bvhi, Behi[ei]], pctx[ob:ob + 64, off:C],
                               vhi[:, vt], ehi[ei][:, esl],
                               start=(kt == 0), stop=False, tile_position=tp)
                            mm([Bpctx], [Bvhi, Belo[ei]], pctx[ob:ob + 64, off:C],
                               vhi[:, vt], elo[ei][:, esl],
                               start=False, stop=False, tile_position=tp)
                            mm([Bpctx], [Bvlo, Behi[ei]], pctx[ob:ob + 64, off:C],
                               vlo[:, vt], ehi[ei][:, esl],
                               start=False, stop=(kt == nkt - 1),
                               tile_position=tp)
                    if ATTN_LVL < 6:
                        continue
                    op("vector", "tensor_copy", [Brs0], [Bpms], rs0[:],
                       pms[0:1, :])
                    op("vector", "tensor_copy", [Brs1], [Bpp[0]], rs1[:],
                       pp[0][0:1, :])
                    op("vector", "reciprocal", [Brs0], [Brs0], rs0[:], rs0[:])
                    op("vector", "reciprocal", [Brs1], [Brs1], rs1[:], rs1[:])
                    mm([BpsA[0]], [Brs0, BmaskA], psA[0][:], maskA[:],
                       rs0[:], start=True, stop=False)
                    mm([BpsA[0]], [Brs1, BmaskB], psA[0][:], maskB[:],
                       rs1[:], start=False, stop=True)
                    op("vector", "tensor_copy", [Brbq], [BpsA[0]], rbq[:],
                       psA[0][:])
                    op("vector", "tensor_tensor", [Bscra], [Bpctx, Brbq],
                       scra[:], pctx[:], rbq[:], OP.mult)
                    split_hilo(scra[:], Bscra, ctxhi[:, j * C:(j + 1) * C],
                               Bctxhi, ctxlo[:, j * C:(j + 1) * C], Bctxlo)
                for o in range(NF if ATTN_WO else 0):
                    par = o % 2
                    for (wsrc, Bwd, wb_, Bwb_) in (
                            (woe["hi"], Bwoe["hi"], wbhi[par], Bwbhi[par]),
                            (woe["lo"], Bwoe["lo"], wblo[par], Bwblo[par])):
                        dma("sync", "sld",
                            wb_[:, 0:8 * 128].rearrange("p (c j) -> p c j", j=128),
                            wsrc[:, o * 128:(o + 1) * 128].rearrange(
                                "(kc pq) jj -> pq kc jj", pq=128),
                            [Bwb_], [Bwd])
                    for kc in range(8):
                        ksl = slice(kc * 128, (kc + 1) * 128)
                        xsl = slice(kc * C, (kc + 1) * C)
                        mm([Bpp[par]], [Bwbhi[par], Bctxhi], pp[par][:],
                           wbhi[par][:, ksl], ctxhi[:, xsl],
                           start=(kc == 0), stop=False)
                        mm([Bpp[par]], [Bwbhi[par], Bctxlo], pp[par][:],
                           wbhi[par][:, ksl], ctxlo[:, xsl], start=False,
                           stop=False)
                        mm([Bpp[par]], [Bwblo[par], Bctxhi], pp[par][:],
                           wblo[par][:, ksl], ctxhi[:, xsl], start=False,
                           stop=(kc == 7))
                    op("vector", "tensor_scalar", [Bscra], [Bpp[par], Bboc],
                       scra[:], pp[par][:], boc[:, o:o + 1], None, OP.add)
                    op("vector", "tensor_tensor", [Bxc], [Bscra, Bxc],
                       xc[:, o * C:(o + 1) * C], xc[:, o * C:(o + 1) * C],
                       scra[:], OP.add)
                for f in range(NF):
                    dma("gpsimd", "sst", xw[f * 128:(f + 1) * 128, sl],
                        xc[:, f * C:(f + 1) * C], [Bxw], [Bxc])

        if "ffn2" in PHASES:
            ffn(final=("exit" in PHASES))

        if "ffn2" not in PHASES:
            for c in range(NCH):
                for f in range(NF):
                    dma("sync", "sld", xc[:, f * C:(f + 1) * C],
                        xw[f * 128:(f + 1) * 128, c * C:(c + 1) * C], [Bxc], [Bxw])
                for f in range(NF):
                    for ts_ in range(4):
                        par2 = ts_ % 2
                        tr([Bpp[par2]], [Bxc, Bidt], pp[par2][:, 0:128],
                           xc[:, f * C + ts_ * 128: f * C + (ts_ + 1) * 128],
                           idt[:])
                        op("vector", "tensor_copy", [Bstage], [Bpp[par2]],
                           stage[:, ts_ * 128:(ts_ + 1) * 128], pp[par2][:, 0:128])
                    for ts_ in range(4):
                        dma("gpsimd", "sst",
                            yout[c * C + ts_ * 128: c * C + (ts_ + 1) * 128,
                                 f * 128:(f + 1) * 128],
                            stage[:, ts_ * 128:(ts_ + 1) * 128], [By], [Bstage])

        def _fin(e):
            for s in ("sld", "sst"):
                if p.cnt.get(s):
                    e.wait_ge(p.sems[s], p.cnt[s])
        p.raw("gpsimd", _fin)

        for s, v in p.cnt.items():
            assert v < 60000, f"semaphore {s} overflow risk: {v}"

        with nc.Block() as block:
            @block.sync
            def _(eng):
                for fn in p.ops["sync"]:
                    fn(eng)

            @block.tensor
            def _(eng):
                p._pad_src("tensor")
                for i, fn in enumerate(p.ops["tensor"]):
                    try:
                        fn(eng)
                    except Exception:
                        print("TENSOR OP FAIL at", i, p.src["tensor"][i] if i < len(p.src["tensor"]) else "?")
                        raise

            @block.vector
            def _(eng):
                for fn in p.ops["vector"]:
                    fn(eng)

            @block.scalar
            def _(eng):
                for fn in p.ops["scalar"]:
                    fn(eng)

            @block.gpsimd
            def _(eng):
                for fn in p.ops["gpsimd"]:
                    fn(eng)

    return nc


_NC_CACHE = [None]


def kernel(**inputs):
    x = np.ascontiguousarray(np.asarray(inputs["L0_attn_x"], dtype=np.float32))
    f32 = lambda a: np.ascontiguousarray(np.asarray(a, dtype=np.float32))
    base = {
        "Wq": f32(inputs["L0_attn_Wq"]),
        "bq": f32(inputs["L0_attn_bq"]).reshape(D, 1),
        "Wl": f32(inputs["L0_attn_Wl"]),
        "bl": f32(inputs["L0_attn_bl"]).reshape(1, LAT),
        "Wk": f32(inputs["L0_attn_Wk"]),
        "bk": f32(inputs["L0_attn_bk"]).reshape(1, NKV * KD),
        "Wv": f32(inputs["L0_attn_Wv"]),
        "bv": f32(inputs["L0_attn_bv"]).reshape(1, NKV * KD),
        "Wq2kv": f32(inputs["L0_attn_Wq2kv"]),
        "Wkv2h": f32(inputs["L0_attn_Wkv2h"]),
        "bkv2h": f32(inputs["L0_attn_bkv2h"]).reshape(HD, 1),
        "Wo": f32(inputs["L0_attn_Wo"]),
        "bo": f32(inputs["L0_attn_bo"]).reshape(1, D),
        "Wf": f32(inputs["L0_ffn_weight"]),
        "gam": f32(inputs["L0_ffn_norm_weight"]).reshape(D, 1),
        "ident": np.eye(128, dtype=np.float32),
        "m0": np.triu(np.ones((128, 128), dtype=np.float32)),
    }
    if _NC_CACHE[0] is None:
        _NC_CACHE[0] = build_nc()
    nc = _NC_CACHE[0]
    in_maps = [dict(base, x_in=x[c]) for c in range(8)]
    res = run_bass_kernel_spmd(nc, in_maps, core_ids=list(range(8)))
    if DEBUG:
        kernel.dbg = [np.asarray(r["dbg"], dtype=np.float32) for r in res.results]
    return np.stack([np.asarray(r["y"], dtype=np.float32) for r in res.results])


# version 05
# speedup vs baseline: 1.0230x; 1.0230x over previous
import contextlib
import numpy as np
import concourse.bass as bass
import concourse.mybir as mybir
from concourse.bass_utils import run_bass_kernel_spmd

F32 = mybir.dt.float32
I32 = mybir.dt.int32
BF16 = mybir.dt.bfloat16
OP = mybir.AluOpType
AF = mybir.ActivationFunctionType
AX = mybir.AxisListType

T = 2048
D = 2048
C = 512
NCH = T // C
NF = D // 128
LAT = 1024
NH, NKV, HD, KD = 16, 4, 128, 64
NPAIR = 8
PAIR_A = [0, 1, 2, 3, 8, 9, 10, 11]

PHASES = ("entry", "prep", "ffn1", "attn", "ffn2", "exit")
DEBUG = False
ATTN_NCH = NCH
ATTN_NPAIR = NPAIR
ATTN_INNER = True
ATTN_LVL = 6
ATTN_WO = True

ENGS = ("sync", "tensor", "vector", "scalar", "gpsimd")
ESEM = {"sync": None, "tensor": "smm", "vector": "sdv", "scalar": "sac",
        "gpsimd": "spl"}


class Buf:
    __slots__ = ("w", "r", "wvi")

    def __init__(self):
        self.w = None
        self.r = []
        self.wvi = -10


class B:

    def __init__(self):
        self.ops = {e: [] for e in ENGS}
        self.src = {e: [] for e in ENGS}
        self.cnt = {}
        self.sems = {}
        self.last_vec = None
        self.sep_ap = None
        self.vec_idx = 0
        self.waited = {e: {} for e in ENGS}

    def _bump(self, s, n):
        self.cnt[s] = self.cnt.get(s, 0) + n
        return self.cnt[s]

    def _pad_src(self, eng):
        pass

    def _waits(self, eng, ins, outs):
        evs = []
        for b in ins:
            if b.w:
                evs.append(b.w)
        for b in outs:
            if b.w:
                evs.append(b.w)
            evs.extend(b.r)
        best = {}
        for s, v, src in evs:
            if src == eng and src is not None:
                continue
            if s not in best or best[s] < v:
                best[s] = v
        for s, v in best.items():
            if self.waited[eng].get(s, -1) >= v:
                continue
            self.waited[eng][s] = v
            self.src[eng].append("wait")
            self.ops[eng].append(lambda e, sm=s, vv=v: e.wait_ge(self.sems[sm], vv))

    def op(self, eng, meth, outs, ins, *args, **kw):
        if eng == "vector" and self.sep_ap is not None and meth != "_sep":
            need = 0
            for b in ins:
                gap = self.vec_idx - b.wvi
                if 0 < gap < 3:
                    need = max(need, 3 - gap)
            for _ in range(need):
                self.op("vector", "_sep", [], [])
        if meth == "_sep":
            eng, meth, args, kw = "vector", "tensor_copy",                 (self.sep_ap, self.sep_ap), {}
        if eng == "vector":
            self.last_vec = meth
        self._waits(eng, ins, outs)
        s = ESEM[eng]
        v = self._bump(s, 1)
        import sys as _s
        fr = _s._getframe(1)
        if fr.f_code.co_name in ("mm", "tr", "op"):
            fr = _s._getframe(2)
        self.src[eng].append(f"{fr.f_lineno}:{meth}")
        self.ops[eng].append(
            lambda e, m=meth, a=args, k2=dict(kw), sm=s:
            getattr(e, m)(*a, **k2).then_inc(self.sems[sm], 1))
        ev = (s, v, eng)
        for b in outs:
            b.w = ev
            b.r = []
            if eng == "vector":
                b.wvi = self.vec_idx
        for b in ins:
            b.r.append(ev)
        if eng == "vector":
            self.vec_idx += 1
        return ev

    def mm(self, outs, ins, *args, **kw):
        kw.setdefault("skip_group_check", True)
        return self.op("tensor", "matmul", outs, ins, *args, **kw)

    def tr(self, outs, ins, *args, **kw):
        return self.op("tensor", "transpose", outs, ins, *args, **kw)

    def dma(self, eng, sname, dst_ap, src_ap, outs, ins, **kw):
        self._waits(eng, ins, outs)
        v = self._bump(sname, 16)
        self.ops[eng].append(
            lambda e, o=dst_ap, i=src_ap, sm=sname, k2=dict(kw):
            e.dma_start(out=o, in_=i, **k2).then_inc(self.sems[sm], 16))
        ev = (sname, v, None)
        for b in outs:
            b.w = ev
            b.r = []
        for b in ins:
            b.r.append(ev)
        return ev

    def collective(self, wouts, wins, *args, **kw):
        self._waits("gpsimd", wins, wouts)
        v = self._bump("scc", 1)
        self.ops["gpsimd"].append(
            lambda e, a=args, k2=dict(kw):
            e.collective_compute(*a, **k2).then_inc(self.sems["scc"]))
        ev = ("scc", v, None)
        for b in wouts:
            b.w = ev
            b.r = []
        for b in wins:
            b.r.append(ev)
        return ev

    def raw(self, eng, fn):
        self.ops[eng].append(fn)


def build_nc():
    nc = bass.Bass("TRN2", target_bir_lowering=False,
                   detect_race_conditions=False)
    p = B()

    xin = nc.dram_tensor("x_in", [T, D], F32, kind="ExternalInput")
    hspec = [
        ("Wq", [D, D]), ("bq", [D, 1]), ("Wl", [D, LAT]), ("bl", [1, LAT]),
        ("Wk", [LAT, NKV * KD]), ("bk", [1, NKV * KD]),
        ("Wv", [LAT, NKV * KD]), ("bv", [1, NKV * KD]),
        ("Wq2kv", [HD, KD]), ("Wkv2h", [KD, HD]), ("bkv2h", [HD, 1]),
        ("Wo", [D, D]), ("bo", [1, D]),
        ("Wf", [D, D]), ("gam", [D, 1]),
        ("ident", [128, 128]), ("m0", [128, 128]),
    ]
    h = {name: nc.dram_tensor(name, shape, F32, kind="ExternalInput")
         for name, shape in hspec}
    yout = nc.dram_tensor("y", [T, D], F32, kind="ExternalOutput")
    dbg = nc.dram_tensor("dbg", [128, 1600], F32, kind="ExternalOutput") if DEBUG else None

    xw = nc.dram_tensor("x_work", [D, T], F32)
    wqTs = nc.dram_tensor("wqT_s", [D, D], BF16)
    wqe = {s: nc.dram_tensor(f"wqe_{s}", [D, NH * KD], BF16) for s in ("hi", "lo")}
    wls = {s: nc.dram_tensor(f"wl_{s}", [D, LAT], BF16) for s in ("hi", "lo")}
    wks = {s: nc.dram_tensor(f"wk_{s}", [LAT, NKV * KD], BF16) for s in ("hi", "lo")}
    wvs = {s: nc.dram_tensor(f"wv_{s}", [LAT, NKV * KD], BF16) for s in ("hi", "lo")}
    woe = {s: nc.dram_tensor(f"woe_{s}", [NH * KD, D], BF16) for s in ("hi", "lo")}
    cc_in = nc.dram_tensor("cc_in", [1, 8], F32)
    cc_out = nc.dram_tensor("cc_out", [1, 8], F32)
    scal_h = nc.dram_tensor("scal_h", [1, 8], F32)
    row_h = nc.dram_tensor("row_h", [1, D], F32)
    rows_h = nc.dram_tensor("rows_h", [2, T], F32)

    es = contextlib.ExitStack()
    with es:
        def sb(name, shape, dt=F32):
            t = es.enter_context(nc.sbuf_tensor(name, shape, dt))
            return t, Buf()

        def psm(name, shape, dt=F32):
            t = es.enter_context(nc.psum_tensor(name, shape, dt))
            return t, Buf()

        for s in ("sld", "sst", "smm", "sdv", "sac", "spl", "scc"):
            p.sems[s] = es.enter_context(nc.semaphore(s))

        xc, Bxc = sb("xc", [128, NF * C])
        abfA, BabfA = sb("abfA", [128, NF * C], BF16)
        abfB, BabfB = sb("abfB", [128, NF * C], BF16)
        qhi, Bqhi = sb("qhi", [128, NPAIR * C], BF16)
        qlo, Bqlo = sb("qlo", [128, NPAIR * C], BF16)
        lathi, Blathi = sb("lathi", [128, 2 * C], BF16)
        latlo, Blatlo = sb("latlo", [128, 2 * C], BF16)
        khi, Bkhi = sb("khi", [128, 2 * T], BF16)
        klo, Bklo = sb("klo", [128, 2 * T], BF16)
        vhi, Bvhi = sb("vhi", [128, (T // 128) * 256], BF16)
        vlo, Bvlo = sb("vlo", [128, (T // 128) * 256], BF16)
        af32, Baf32 = sb("af32", [128, 4 * C])
        ehi, Behi = zip(*[sb(f"ehi{i}", [128, C], BF16) for i in range(4)])
        elo, Belo = zip(*[sb(f"elo{i}", [128, C], BF16) for i in range(4)])
        wrow, Bwrow = sb("wrow", [128, D])
        wbf, Bwbf = sb("wbf", [128, D], BF16)
        wbf2, Bwbf2 = sb("wbf2", [128, D], BF16)
        wbhi, Bwbhi = zip(*[sb(f"wbhi{i}", [128, NF * 128], BF16) for i in range(2)])
        wblo, Bwblo = zip(*[sb(f"wblo{i}", [128, NF * 128], BF16) for i in range(2)])
        kvwh, Bkvwh = sb("kvwh", [128, 4 * 128], BF16)
        kvwl, Bkvwl = sb("kvwl", [128, 4 * 128], BF16)

        t1, Bt1 = sb("t1", [128, C])
        t3, Bt3 = sb("t3", [128, C])
        scri, Bscri = sb("scri", [128, C], I32)
        scra, Bscra = sb("scra", [128, C])
        scrb, Bscrb = sb("scrb", [128, C])
        scrc, Bscrc = sb("scrc", [128, C])
        mb, Bmb = sb("mb", [128, C])
        rb, Brb = sb("rb", [128, C])
        rs0, Brs0 = sb("rs0", [1, C])
        rs1, Brs1 = sb("rs1", [1, C])
        stage, Bstage = scrc, Bscrc
        m_r, Bm_r = sb("m_r", [1, C])
        r_r, Br_r = sb("r_r", [1, C])
        v_r, Bv_r = rs1[0:1, :], Brs1
        s_r, Bs_r = rs0[0:1, :], Brs0
        brow, Bbrow = rs0[0:1, :], Brs0
        idt, Bidt = sb("idt", [128, 128])
        m0s, Bm0 = sb("m0s", [128, 128])
        onesc, Bonesc = sb("onesc", [128, 1])
        onescbf, Bonescbf = sb("onescbf", [128, 1], BF16)
        onesr, Bonesr = sb("onesr", [1, 128])
        ones33, Bones33 = sb("ones33", [128, 33])
        maskA, BmaskA = sb("maskA", [1, 128])
        maskB, BmaskB = sb("maskB", [1, 128])
        gcol, Bgcol = sb("gcol", [128, NF])
        q2kvh, Bq2kvh = sb("q2kvh", [128, KD], BF16)
        q2kvl, Bq2kvl = sb("q2kvl", [128, KD], BF16)
        kv2h_h, Bkv2h_h = sb("kv2h_h", [128, KD], BF16)
        kv2h_l, Bkv2h_l = sb("kv2h_l", [128, KD], BF16)
        bqcolh, Bbqcolh = sb("bqcolh", [128, 1], BF16)
        bqcoll, Bbqcoll = sb("bqcoll", [128, 1], BF16)
        bqc, Bbqc = sb("bqc", [128, 8])
        blc, Bblc = sb("blc", [128, 8])
        bkc, Bbkc = sb("bkc", [128, 2])
        bvc, Bbvc = sb("bvc", [128, 2])
        boc, Bboc = sb("boc", [128, NF])
        am, Bam = sb("am", [128, 1])
        s8, Bs8 = sb("s8", [1, 8])
        sc1, Bsc1 = sb("sc1", [1, 1])
        sc2, Bsc2 = sb("sc2", [1, 1])
        wsc, Bwsc = sb("wsc", [1, 1])
        rho, Brho = sb("rho", [128, 1])
        ahp, Bahp = sb("ahp", [128, 1])
        rwp, Brwp = sb("rwp", [128, 1])
        wscp, Bwscp = sb("wscp", [128, 1])
        mxc, Bmxc = sb("mxc", [128, 1])
        scp, Bscp = sb("scp", [128, 1])
        tmpc, Btmpc = sb("tmpc", [128, 1])
        sep, _Bsep = sb("sep", [128, 1])
        kmax2, Bkmax2 = sb("kmax2", [33, 2])
        qm2, Bqm2 = sb("qm2", [33, 8])
        bpair, Bbpair = sb("bpair", [33, 1])
        bnegT, BbnegT = sb("bnegT", [1, 33])
        biasA, BbiasA = sb("biasA", [128, 1])
        biasB, BbiasB = sb("biasB", [128, 1])

        rnd, Brnd = abfA, BabfA
        xnhi, Bxnhi = abfA, BabfA
        xnlo, Bxnlo = abfB, BabfB
        ctxhi = abfB[:, 0:NPAIR * C]
        ctxlo = abfB[:, NPAIR * C:2 * NPAIR * C]
        Bctxhi = Bctxlo = BabfB
        xtile, Bxtile = wrow, Bwrow
        wrow2 = af32
        Bwrow2 = Baf32
        efb = [sb(f"efb{i}", [128, C]) for i in range(4)]
        ef = [t for t, _ in efb]
        Bef = [b for _, b in efb]
        obf = wbhi[0][0:64, :]
        obf2 = wblo[0][0:64, :]
        Bobf, Bobf2 = Bwbhi[0], Bwblo[0]
        rbq, Brbq = mb, Bmb

        pp, Bpp = zip(*[psm(f"pp{i}", [128, C]) for i in range(2)])
        psA, BpsA = zip(*[psm(f"psA{i}", [128, C]) for i in range(2)])
        psB, BpsB = zip(*[psm(f"psB{i}", [128, C]) for i in range(2)])
        pctx, Bpctx = psm("pctx", [128, C])
        pms, Bpms = psm("pms", [128, C])

        Bscal = [Buf() for _ in range(8)]
        Bxw = Buf()
        BwqTs = Buf()
        Bwqe = {s: Buf() for s in ("hi", "lo")}
        Bwls = {s: Buf() for s in ("hi", "lo")}
        Bwks = {s: Buf() for s in ("hi", "lo")}
        Bwvs = {s: Buf() for s in ("hi", "lo")}
        Bwoe = {s: Buf() for s in ("hi", "lo")}
        Bccin, Bccout = Buf(), Buf()
        Bxin, By = Buf(), Buf()
        Browh = Buf()
        Browsh = Buf()
        Bh = {k: Buf() for k in h}

        p.sep_ap = sep[:]
        op = p.op
        mm = p.mm
        tr = p.tr
        dma = p.dma

        def split_hilo(src, Bsrc, dsthi, Bdh, dstlo, Bdl):
            op("vector", "tensor_copy", [Bdh], [Bsrc], dsthi, src)
            op("vector", "tensor_tensor", [Bdl], [Bsrc, Bdh], dstlo, src, dsthi,
               OP.subtract)

        def round_bf16(src, Bsrc, dst, Bdst, width):
            si = scri[:, 0:width]
            sa = scra[:, 0:width]
            sb_ = scrb[:, 0:width]
            op("vector", "tensor_copy", [Bscri], [Bsrc], si, src)
            op("vector", "scalar_tensor_tensor", [Bscra], [Bsrc, Bscri], sa, src,
               0.5, si, OP.add, OP.subtract)
            op("vector", "tensor_scalar", [Bscrb], [Bscra], sb_, sa, 1.0, None,
               OP.is_gt)
            op("vector", "scalar_tensor_tensor", [Bscra], [Bscra, Bscrb], sa, sa,
               0.0, sb_, OP.is_lt, OP.subtract)
            op("vector", "tensor_tensor", [Bdst], [Bscri, Bscra], dst, si, sa,
               OP.subtract)

        def tree(t, Bt, p_, col0, w, opk):
            while w > 1:
                w //= 2
                op("vector", "tensor_tensor", [Bt], [Bt],
                   t[0:p_, col0:col0 + w], t[0:p_, col0:col0 + w],
                   t[0:p_, col0 + w:col0 + 2 * w], opk)

        def bcast_scal(src_ap, Bsrc, dst, Bdst, lane):
            dma("gpsimd", "sst", scal_h[0:1, lane:lane + 1], src_ap,
                [Bscal[lane]], [Bsrc])
            dma("gpsimd", "sst", dst,
                scal_h[0:1, lane:lane + 1].broadcast_to((128, 1)),
                [Bdst], [Bscal[lane]])

        def cols_from_rowh(n, dstcols, Bdst):
            dma("gpsimd", "sst", dstcols,
                row_h[0:1, 0:n * 128].rearrange("a (f q) -> (a q) f", q=128),
                [Bdst], [Browh], allow_slow_non_contiguous=True)

        def bcast_row(src_ap, Bsrc, dst, Bdst):
            mm([Bpms], [Bsrc, Bonesr], pms[:], onesr[:], src_ap, start=True,
               stop=True)
            op("vector", "tensor_copy", [Bdst], [Bpms], dst, pms[:])

        op("vector", "memset", [], [], sep[:], 0.0)
        op("vector", "memset", [Bonesc], [], onesc[:], 1.0)
        op("vector", "memset", [Bonesr], [], onesr[:], 1.0)
        op("vector", "memset", [Bonescbf], [], onescbf[:], 1.0)
        op("vector", "memset", [Bones33], [], ones33[:], 0.0)
        op("vector", "memset", [Bones33], [], ones33[0:64, 0:1], 1.0)
        op("vector", "memset", [Bones33], [], ones33[64:128, 32:33], 1.0)
        op("vector", "memset", [BmaskA], [], maskA[:, 0:64], 1.0)
        op("vector", "memset", [BmaskA], [], maskA[:, 64:128], 0.0)
        op("vector", "memset", [BmaskB], [], maskB[:, 0:64], 0.0)
        op("vector", "memset", [BmaskB], [], maskB[:, 64:128], 1.0)
        dma("sync", "sld", idt[:], h["ident"].ap(), [Bidt], [])
        dma("sync", "sld", m0s[:], h["m0"].ap(), [Bm0], [])
        for f in range(NF):
            dma("sync", "sld", gcol[:, f:f + 1], h["gam"][f * 128:(f + 1) * 128, 0:1],
                [Bgcol], [])

        if "entry" in PHASES:
            for tt in range(T // 128):
                dma("sync", "sld", xtile[:], xin[tt * 128:(tt + 1) * 128, :],
                    [Bxtile], [Bxin])
                for g in range(4):
                    Bp = Bpp[g % 2]
                    for j in range(4):
                        f = g * 4 + j
                        tr([Bp], [Bxtile, Bidt], pp[g % 2][:, j * 128:(j + 1) * 128],
                           xtile[:, f * 128:(f + 1) * 128], idt[:])
                    op("vector", "tensor_copy", [Bstage], [Bp], stage[:], pp[g % 2][:])
                    for j in range(4):
                        f = g * 4 + j
                        dma("gpsimd", "sst",
                            xw[f * 128:(f + 1) * 128, tt * 128:(tt + 1) * 128],
                            stage[:, j * 128:(j + 1) * 128], [Bxw], [Bstage])

        if "prep" in PHASES:
            dma("sync", "sld", wrow[:, 0:KD], h["Wq2kv"].ap(), [Bwrow], [Bh["Wq2kv"]])
            split_hilo(wrow[:, 0:KD], Bwrow, q2kvh[:], Bq2kvh, q2kvl[:], Bq2kvl)
            dma("sync", "sld", wrow[0:KD, 128:256], h["Wkv2h"].ap(), [Bwrow],
                [Bh["Wkv2h"]])
            tr([Bpp[0]], [Bwrow, Bidt], pp[0][:, 0:KD], wrow[0:KD, 128:256],
               idt[0:KD, 0:KD])
            split_hilo(pp[0][:, 0:KD], Bpp[0], kv2h_h[:], Bkv2h_h, kv2h_l[:],
                       Bkv2h_l)

            accw = xc
            for o in range(NF):
                dma("sync", "sld", wrow[:], h["Wf"][o * 128:(o + 1) * 128, :],
                    [Bwrow], [Bh["Wf"]])
                op("scalar", "activation", [Bscrc], [Bwrow], scrc[:],
                   wrow[:, 0:C], AF.Abs)
                op("scalar", "activation", [Bscra], [Bwrow], scra[:],
                   wrow[:, C:2 * C], AF.Abs)
                op("scalar", "activation", [Bscrb], [Bwrow], scrb[:],
                   wrow[:, 2 * C:3 * C], AF.Abs)
                op("scalar", "activation", [Bt3], [Bwrow], t3[:],
                   wrow[:, 3 * C:4 * C], AF.Abs)
                for (buf, Bbuf, q4) in ((scrc, Bscrc, 0), (scra, Bscra, 1),
                                        (scrb, Bscrb, 2), (t3, Bt3, 3)):
                    if o == 0:
                        op("vector", "tensor_copy", [Bxc], [Bbuf],
                           accw[:, q4 * C:(q4 + 1) * C], buf[:])
                    else:
                        op("vector", "tensor_tensor", [Bxc], [Bxc, Bbuf],
                           accw[:, q4 * C:(q4 + 1) * C],
                           accw[:, q4 * C:(q4 + 1) * C], buf[:], OP.add)
            op("vector", "tensor_tensor", [Bxc], [Bxc], accw[:, 0:2 * C],
               accw[:, 0:2 * C], accw[:, 2 * C:4 * C], OP.add)
            tree(accw, Bxc, 128, 0, 2 * C, OP.add)
            dma("gpsimd", "sst", row_h[0:1, 0:128], accw[:, 0:1], [Browh], [Bxc])
            dma("gpsimd", "sst", scrc[:, 0:128],
                row_h[0:1, 0:128].broadcast_to((128, 128)), [Bscrc], [Browh])
            tree(scrc, Bscrc, 128, 0, 128, OP.add)
            op("vector", "tensor_scalar", [Bwscp], [Bscrc], wscp[:],
               scrc[:, 0:1], 1.0 / (D * D), 1e-5, OP.mult, OP.max)
            op("vector", "reciprocal", [Brwp], [Bwscp], rwp[:], wscp[:])
            if DEBUG:
                Bdbg = Buf()
                dma("gpsimd", "sst", dbg[:, 1570:1571], am[:], [Bdbg], [Bam],
                    allow_slow_non_contiguous=True)
                dma("gpsimd", "sst", dbg[:, 1571:1572], wscp[:], [Bdbg], [Bwscp],
                    allow_slow_non_contiguous=True)
                dma("gpsimd", "sst", dbg[:, 1573:1574], rwp[:], [Bdbg], [Brwp],
                    allow_slow_non_contiguous=True)

            for o in range(NF):
                dma("sync", "sld", wrow[:], h["Wf"][o * 128:(o + 1) * 128, :],
                    [Bwrow], [Bh["Wf"]])
                for q4 in range(4):
                    sl = slice(q4 * C, (q4 + 1) * C)
                    op("vector", "tensor_scalar", [Bwrow2], [Bwrow, Brwp],
                       wrow2[:, 0:C], wrow[:, sl], rwp[:], None, OP.mult)
                    op("vector", "tensor_scalar", [Bwrow2], [Bwrow2], wrow2[:, 0:C],
                       wrow2[:, 0:C], -1.0, 1.0, OP.max, OP.min)
                    round_bf16(wrow2[:, 0:C], Bwrow2, wbf[:, sl], Bwbf, C)
                for g in range(4):
                    Bp = Bpp[g % 2]
                    for j in range(4):
                        f = g * 4 + j
                        op("vector", "tensor_copy", [Bscrc], [Bwbf],
                           scrc[:, j * 128:(j + 1) * 128],
                           wbf[:, f * 128:(f + 1) * 128])
                    for j in range(4):
                        tr([Bp], [Bscrc, Bidt], pp[g % 2][:, j * 128:(j + 1) * 128],
                           scrc[:, j * 128:(j + 1) * 128], idt[:])
                    op("vector", "tensor_copy", [Bwbf2], [Bp],
                       wbf2[:, g * C:(g + 1) * C], pp[g % 2][:])
                for f in range(NF):
                    dma("gpsimd", "sst",
                        wqTs[f * 128:(f + 1) * 128, o * 128:(o + 1) * 128],
                        wbf2[:, f * 128:(f + 1) * 128], [BwqTs], [Bwbf2])

            for j in range(NPAIR):
                for ab in range(2):
                    hh = PAIR_A[j] + 4 * ab
                    dst = j * 128 + ab * 64
                    dma("sync", "sld",
                        wrow[:].rearrange("p (c j) -> p c j", j=128),
                        h["Wq"][:, hh * 128:(hh + 1) * 128].rearrange(
                            "(c pq) jj -> pq c jj", pq=128),
                        [Bwrow], [Bh["Wq"]])
                    for g in range(4):
                        Bp = Bpp[g % 2]
                        for jj in range(4):
                            f = g * 4 + jj
                            tr([Bp], [Bwrow, Bidt],
                               pp[g % 2][:, jj * 128:(jj + 1) * 128],
                               wrow[:, f * 128:(f + 1) * 128], idt[:])
                        split_hilo(pp[g % 2][:], Bp, wbf[:, g * C:(g + 1) * C],
                                   Bwbf, wbf2[:, g * C:(g + 1) * C], Bwbf2)
                    for half in range(2):
                        Bp = Bpp[half]
                        for cc in range(8):
                            c = half * 8 + cc
                            bsl = slice(c * 128, (c + 1) * 128)
                            osl = slice(cc * KD, (cc + 1) * KD)
                            mm([Bp], [Bwbf, Bq2kvh], pp[half][:, osl],
                               wbf[:, bsl], q2kvh[:], start=True, stop=False)
                            mm([Bp], [Bwbf, Bq2kvl], pp[half][:, osl],
                               wbf[:, bsl], q2kvl[:], start=False, stop=False)
                            mm([Bp], [Bwbf2, Bq2kvh], pp[half][:, osl],
                               wbf2[:, bsl], q2kvh[:], start=False, stop=True)
                        split_hilo(pp[half][:], Bp, wbf[:, half * C:(half + 1) * C],
                                   Bwbf, wbf2[:, half * C:(half + 1) * C], Bwbf2)
                        for cc in range(8):
                            c = half * 8 + cc
                            dma("gpsimd", "sst",
                                wqe["hi"][c * 128:(c + 1) * 128, dst:dst + KD],
                                wbf[:, half * C + cc * KD: half * C + (cc + 1) * KD],
                                [Bwqe["hi"]], [Bwbf])
                            dma("gpsimd", "sst",
                                wqe["lo"][c * 128:(c + 1) * 128, dst:dst + KD],
                                wbf2[:, half * C + cc * KD: half * C + (cc + 1) * KD],
                                [Bwqe["lo"]], [Bwbf2])

            dma("sync", "sld", wrow[0:1, 0:128] if False else wrow[0:128, 0:1],
                h["bkv2h"].ap(), [Bwrow], [Bh["bkv2h"]])
            op("vector", "tensor_copy", [Bscrc], [Bwrow], scrc[:, 0:1],
               wrow[0:128, 0:1])
            nh_i = 0
            for j in range(NPAIR):
                for ab in range(2):
                    hh = PAIR_A[j] + 4 * ab
                    dst = j * 128 + ab * 64
                    dma("sync", "sld", wrow[:], h["Wo"][hh * 128:(hh + 1) * 128, :],
                        [Bwrow], [Bh["Wo"]])
                    split_hilo(wrow[:], Bwrow, wbf[:], Bwbf, wbf2[:], Bwbf2)
                    for cc in range(4):
                        sl = slice(cc * C, (cc + 1) * C)
                        Bp = Bpp[cc % 2]
                        mm([Bp], [Bkv2h_h, Bwbf], pp[cc % 2][0:KD, :],
                           kv2h_h[:], wbf[:, sl], start=True, stop=False)
                        mm([Bp], [Bkv2h_l, Bwbf], pp[cc % 2][0:KD, :],
                           kv2h_l[:], wbf[:, sl], start=False, stop=False)
                        mm([Bp], [Bkv2h_h, Bwbf2], pp[cc % 2][0:KD, :],
                           kv2h_h[:], wbf2[:, sl], start=False, stop=True)
                        mm([Bpms], [Bscrc, Bwrow], pms[32 * cc:32 * cc + 1, :],
                           scrc[:, 0:1], wrow[:, sl],
                           start=(nh_i == 0), stop=(nh_i == NH - 1),
                           tile_position=(0, 32 * cc))
                        split_hilo(pp[cc % 2][0:KD, :], Bp, obf[:, sl], Bobf,
                                   obf2[:, sl], Bobf2)
                    dma("gpsimd", "sst", woe["hi"][dst:dst + KD, :], obf[:],
                        [Bwoe["hi"]], [Bobf])
                    dma("gpsimd", "sst", woe["lo"][dst:dst + KD, :], obf2[:],
                        [Bwoe["lo"]], [Bobf2])
                    nh_i += 1
            for cc in range(4):
                dma("sync", "sld", brow[:], h["bo"][0:1, cc * C:(cc + 1) * C],
                    [Bbrow], [Bh["bo"]])
                op("vector", "tensor_tensor", [Bbrow], [Bpms, Bbrow], brow[:],
                   pms[32 * cc:32 * cc + 1, :], brow[:], OP.add)
                dma("gpsimd", "sst", row_h[0:1, cc * C:(cc + 1) * C], brow[:],
                    [Browh], [Bbrow])
            cols_from_rowh(NF, boc[:], Bboc)

            for j in range(NPAIR):
                for ab in range(2):
                    hh = PAIR_A[j] + 4 * ab
                    dma("sync", "sld", wrow[0:128, 0:1],
                        h["bq"][hh * 128:(hh + 1) * 128, 0:1], [Bwrow], [Bh["bq"]])
                    split_hilo(wrow[0:128, 0:1], Bwrow, bqcolh[:], Bbqcolh,
                               bqcoll[:], Bbqcoll)
                    mm([Bpms], [Bbqcolh, Bq2kvh], pms[0:1, 0:KD], bqcolh[:],
                       q2kvh[:], start=True, stop=False)
                    mm([Bpms], [Bbqcolh, Bq2kvl], pms[0:1, 0:KD], bqcolh[:],
                       q2kvl[:], start=False, stop=False)
                    mm([Bpms], [Bbqcoll, Bq2kvh], pms[0:1, 0:KD], bqcoll[:],
                       q2kvh[:], start=False, stop=True)
                    op("vector", "tensor_copy", [Bbrow], [Bpms], brow[0:1, 0:KD],
                       pms[0:1, 0:KD])
                    dma("gpsimd", "sst",
                        row_h[0:1, j * 128 + ab * 64: j * 128 + ab * 64 + KD],
                        brow[0:1, 0:KD], [Browh], [Bbrow])
            cols_from_rowh(8, bqc[:], Bbqc)

            for (src, Bsrc_, n, dstc, Bdstc) in (
                    (h["bl"], Bh["bl"], 8, blc, Bblc),
                    (h["bk"], Bh["bk"], 2, bkc, Bbkc),
                    (h["bv"], Bh["bv"], 2, bvc, Bbvc)):
                dma("gpsimd", "sst", row_h[0:1, 0:n * 128], src[0:1, 0:n * 128],
                    [Browh], [Bsrc_])
                cols_from_rowh(n, dstc[:], Bdstc)

            for o in range(NF):
                dma("sync", "sld", wrow[:, 0:LAT], h["Wl"][o * 128:(o + 1) * 128, :],
                    [Bwrow], [Bh["Wl"]])
                split_hilo(wrow[:, 0:LAT], Bwrow, wbf[:, 0:LAT], Bwbf,
                           wbf2[:, 0:LAT], Bwbf2)
                dma("gpsimd", "sst", wls["hi"][o * 128:(o + 1) * 128, :],
                    wbf[:, 0:LAT], [Bwls["hi"]], [Bwbf])
                dma("gpsimd", "sst", wls["lo"][o * 128:(o + 1) * 128, :],
                    wbf2[:, 0:LAT], [Bwls["lo"]], [Bwbf2])
            for (mat, Bmat, dsts, Bd) in ((h["Wk"], Bh["Wk"], wks, Bwks),
                                          (h["Wv"], Bh["Wv"], wvs, Bwvs)):
                for o in range(8):
                    dma("sync", "sld", wrow[:, 0:256], mat[o * 128:(o + 1) * 128, :],
                        [Bwrow], [Bmat])
                    split_hilo(wrow[:, 0:256], Bwrow, wbf[:, 0:256], Bwbf,
                               wbf2[:, 0:256], Bwbf2)
                    dma("gpsimd", "sst", dsts["hi"][o * 128:(o + 1) * 128, :],
                        wbf[:, 0:256], [Bd["hi"]], [Bwbf])
                    dma("gpsimd", "sst", dsts["lo"][o * 128:(o + 1) * 128, :],
                        wbf2[:, 0:256], [Bd["lo"]], [Bwbf2])

        def ln_chunk_stats(extra_rms):
            for f in range(NF):
                op("scalar", "activation", [Bscra], [Bxc], scra[:],
                   xc[:, f * C:(f + 1) * C], AF.Square)
                mm([Bpms], [Bxc, Bonesc], pms[0:1, :], onesc[:],
                   xc[:, f * C:(f + 1) * C], start=(f == 0), stop=(f == NF - 1))
                mm([Bpctx], [Bscra, Bonesc], pctx[0:1, :], onesc[:], scra[:],
                   start=(f == 0), stop=(f == NF - 1))
            op("vector", "tensor_scalar", [Bm_r], [Bpms], m_r[:], pms[0:1, :],
               1.0 / D, None, OP.mult)
            op("vector", "tensor_scalar", [Bv_r], [Bpctx], v_r[:], pctx[0:1, :],
               1.0 / D, None, OP.mult)
            op("vector", "tensor_tensor", [Bs_r], [Bm_r], s_r[:], m_r[:], m_r[:],
               OP.mult)
            op("vector", "tensor_tensor", [Bv_r], [Bv_r, Bs_r], v_r[:], v_r[:],
               s_r[:], OP.subtract)
            if extra_rms:
                op("vector", "tensor_scalar", [Bv_r], [Bv_r], v_r[:], v_r[:],
                   1.0 + 1e-6, 1e-11, OP.mult, OP.add)
            else:
                op("vector", "tensor_scalar", [Bv_r], [Bv_r], v_r[:], v_r[:],
                   1e-5, None, OP.add)
            op("vector", "reciprocal", [Bs_r], [Bv_r], s_r[:], v_r[:])
            op("scalar", "activation", [Br_r], [Bs_r], r_r[:], s_r[:], AF.Sqrt)

        ffn_calls = [0]

        def ffn_dbg(c, o):
            if not DEBUG or ffn_calls[0] != 1 or c != 0 or o != 0:
                return
            Bdbg = Buf()
            op("vector", "tensor_copy", [Bscrb], [Brnd], scrb[:], rnd[:, 0:C])
            dma("gpsimd", "sst", dbg[:, 0:C], scrb[:], [Bdbg], [Bscrb])
            op("vector", "tensor_copy", [Bscrb], [Bwbhi[0]], scrb[:],
               wbhi[0][:, 0:C])
            dma("gpsimd", "sst", dbg[:, C:2 * C], scrb[:], [Bdbg], [Bscrb])
            dma("gpsimd", "sst", dbg[:, 2 * C:3 * C], xc[:, 0:C], [Bdbg], [Bxc])
            dma("gpsimd", "sst", dbg[:, 1536:1537], rho[:], [Bdbg], [Brho], allow_slow_non_contiguous=True)
            dma("gpsimd", "sst", dbg[:, 1537:1538], ahp[:], [Bdbg], [Bahp], allow_slow_non_contiguous=True)
            dma("gpsimd", "sst", dbg[:, 1538:1539], rwp[:], [Bdbg], [Brwp], allow_slow_non_contiguous=True)
            dma("gpsimd", "sst", dbg[:, 1539:1540], am[:], [Bdbg], [Bam], allow_slow_non_contiguous=True)
            dma("gpsimd", "sst", dbg[:, 1540:1541], wscp[:], [Bdbg], [Bwscp],
                allow_slow_non_contiguous=True)

        def ffn(final):
            ffn_calls[0] += 1
            axmx = af32[:, 0:C]
            axmn = af32[:, C:2 * C]
            op("vector", "memset", [Baf32], [], axmx[:, :], -3.0e38)
            op("vector", "memset", [Baf32], [], axmn[:, :], 3.0e38)
            for c in range(NCH):
                sl = slice(c * C, (c + 1) * C)
                for f in range(NF):
                    dma("sync", "sld", xc[:, f * C:(f + 1) * C],
                        xw[f * 128:(f + 1) * 128, sl], [Bxc], [Bxw])
                ln_chunk_stats(extra_rms=True)
                dma("gpsimd", "sst", rows_h[0:1, sl], m_r[:], [Browsh], [Bm_r])
                dma("gpsimd", "sst", rows_h[1:2, sl], r_r[:], [Browsh], [Br_r])
                bcast_row(m_r[:], Bm_r, mb[:], Bmb)
                bcast_row(r_r[:], Br_r, rb[:], Brb)
                for f in range(NF):
                    op("vector", "tensor_tensor", [Bt1], [Bxc, Bmb], t1[:],
                       xc[:, f * C:(f + 1) * C], mb[:], OP.subtract)
                    op("vector", "scalar_tensor_tensor", [Bt3], [Bt1, Bgcol, Brb],
                       t3[:], t1[:], gcol[:, f:f + 1], rb[:], OP.mult, OP.mult)
                    op("vector", "tensor_tensor", [Baf32], [Baf32, Bt3],
                       axmx[:, :], axmx[:, :], t3[:], OP.max)
                    op("vector", "tensor_tensor", [Baf32], [Baf32, Bt3],
                       axmn[:, :], axmn[:, :], t3[:], OP.min)
            tree(af32, Baf32, 128, 0, C, OP.max)
            tree(af32, Baf32, 128, C, C, OP.min)
            op("vector", "tensor_scalar", [Btmpc], [Baf32], tmpc[:],
               af32[:, C:C + 1], -1.0, None, OP.mult)
            op("vector", "tensor_tensor", [Bmxc], [Baf32, Btmpc], mxc[:],
               af32[:, 0:1], tmpc[:], OP.max)
            dma("gpsimd", "sst", row_h[0:1, 0:128], mxc[:, 0:1], [Browh], [Bmxc])
            dma("gpsimd", "sst", scrc[:, 0:128],
                row_h[0:1, 0:128].broadcast_to((128, 128)), [Bscrc], [Browh])
            tree(scrc, Bscrc, 128, 0, 128, OP.max)
            op("vector", "tensor_copy", [Bmxc], [Bscrc], mxc[:], scrc[:, 0:1])
            dma("gpsimd", "sst", cc_in[0:1, :], mxc[0:8, 0:1], [Bccin], [Bmxc])
            p.collective([Bccout], [Bccin], "AllReduce", OP.max,
                         replica_groups=[list(range(8))],
                         ins=[cc_in.ap().opt()], outs=[cc_out.ap().opt()])
            dma("gpsimd", "sst", scp[:],
                cc_out[0:1, 0:1].broadcast_to((128, 1)), [Bscp], [Bccout])
            op("vector", "tensor_scalar", [Btmpc], [Bscp], tmpc[:], scp[:],
               1e-5, None, OP.max)
            op("vector", "tensor_copy", [Bscp], [Btmpc], scp[:], tmpc[:])
            op("vector", "reciprocal", [Btmpc], [Bscp], tmpc[:], scp[:])
            op("vector", "tensor_scalar", [Brho], [Btmpc], rho[:], tmpc[:],
               127.0, None, OP.mult)
            op("vector", "tensor_scalar", [Btmpc], [Bscp], tmpc[:], scp[:],
               0.5 / 127.0, None, OP.mult)
            op("vector", "tensor_tensor", [Bahp], [Btmpc, Bwscp], ahp[:],
               tmpc[:], wscp[:], OP.mult)
            if DEBUG and ffn_calls[0] == 1:
                Bdbg = Buf()
                dma("gpsimd", "sst", dbg[:, 1543:1544], scp[:], [Bdbg], [Bscp],
                    allow_slow_non_contiguous=True)
                dma("gpsimd", "sst", dbg[:, 1544:1545], rho[:], [Bdbg], [Brho],
                    allow_slow_non_contiguous=True)

            for c in range(NCH):
                sl = slice(c * C, (c + 1) * C)
                for f in range(NF):
                    dma("sync", "sld", xc[:, f * C:(f + 1) * C],
                        xw[f * 128:(f + 1) * 128, sl], [Bxc], [Bxw])
                dma("sync", "sld", m_r[:], rows_h[0:1, sl], [Bm_r], [Browsh])
                dma("sync", "sld", r_r[:], rows_h[1:2, sl], [Br_r], [Browsh])
                bcast_row(m_r[:], Bm_r, mb[:], Bmb)
                bcast_row(r_r[:], Br_r, rb[:], Brb)
                op("vector", "tensor_scalar", [Brb], [Brb, Brho], rb[:], rb[:],
                   rho[:], None, OP.mult)
                for f in range(NF):
                    op("vector", "tensor_tensor", [Bt1], [Bxc, Bmb], t1[:],
                       xc[:, f * C:(f + 1) * C], mb[:], OP.subtract)
                    op("vector", "scalar_tensor_tensor", [Bt3], [Bt1, Bgcol, Brb],
                       t3[:], t1[:], gcol[:, f:f + 1], rb[:], OP.mult, OP.mult)
                    op("vector", "tensor_scalar", [Bt3], [Bt3], t3[:], t3[:],
                       -128.0, 127.0, OP.max, OP.min)
                    round_bf16(t3[:], Bt3, rnd[:, f * C:(f + 1) * C], Brnd, C)
                for o in range(NF):
                    par = o % 2
                    dma("sync", "sld",
                        wbhi[par][:].rearrange("p (c j) -> p c j", j=128),
                        wqTs[:, o * 128:(o + 1) * 128].rearrange(
                            "(kc pq) jj -> pq kc jj", pq=128),
                        [Bwbhi[par]], [BwqTs])
                    for kc in range(NF):
                        mm([Bpp[par]], [Bwbhi[par], Brnd], pp[par][:],
                           wbhi[par][:, kc * 128:(kc + 1) * 128],
                           rnd[:, kc * C:(kc + 1) * C],
                           start=(kc == 0), stop=(kc == NF - 1))
                    op("vector", "scalar_tensor_tensor", [Bxc],
                       [Bpp[par], Bahp, Bxc], xc[:, o * C:(o + 1) * C],
                       pp[par][:], ahp[:], xc[:, o * C:(o + 1) * C],
                       OP.mult, OP.add)
                    ffn_dbg(c, o)
                if not final:
                    for f in range(NF):
                        dma("gpsimd", "sst", xw[f * 128:(f + 1) * 128, sl],
                            xc[:, f * C:(f + 1) * C], [Bxw], [Bxc])
                else:
                    for f in range(NF):
                        for ts_ in range(4):
                            par2 = ts_ % 2
                            tr([Bpp[par2]], [Bxc, Bidt], pp[par2][:, 0:128],
                               xc[:, f * C + ts_ * 128: f * C + (ts_ + 1) * 128],
                               idt[:])
                            op("vector", "tensor_copy", [Bstage], [Bpp[par2]],
                               stage[:, ts_ * 128:(ts_ + 1) * 128],
                               pp[par2][:, 0:128])
                        for ts_ in range(4):
                            dma("gpsimd", "sst",
                                yout[c * C + ts_ * 128: c * C + (ts_ + 1) * 128,
                                     f * 128:(f + 1) * 128],
                                stage[:, ts_ * 128:(ts_ + 1) * 128], [By],
                                [Bstage])

        if "ffn1" in PHASES:
            ffn(final=False)

        if "attn" in PHASES:
            op("vector", "memset", [Bkmax2], [], kmax2[:], 0.0)
            for qc in range(ATTN_NCH):
                sl = slice(qc * C, (qc + 1) * C)
                for f in range(NF):
                    dma("sync", "sld", xc[:, f * C:(f + 1) * C],
                        xw[f * 128:(f + 1) * 128, sl], [Bxc], [Bxw])
                ln_chunk_stats(extra_rms=False)
                bcast_row(m_r[:], Bm_r, mb[:], Bmb)
                bcast_row(r_r[:], Br_r, rb[:], Brb)
                for f in range(NF):
                    op("vector", "tensor_tensor", [Bt1], [Bxc, Bmb], t1[:],
                       xc[:, f * C:(f + 1) * C], mb[:], OP.subtract)
                    op("vector", "tensor_tensor", [Bt3], [Bt1, Brb], t3[:],
                       t1[:], rb[:], OP.mult)
                    split_hilo(t3[:], Bt3, xnhi[:, f * C:(f + 1) * C], Bxnhi,
                               xnlo[:, f * C:(f + 1) * C], Bxnlo)
                for o in range(NPAIR):
                    par = o % 2
                    for (wsrc, Bwd, wb_, Bwb_) in (
                            (wqe["hi"], Bwqe["hi"], wbhi[par], Bwbhi[par]),
                            (wqe["lo"], Bwqe["lo"], wblo[par], Bwblo[par])):
                        dma("sync", "sld",
                            wb_[:].rearrange("p (c j) -> p c j", j=128),
                            wsrc[:, o * 128:(o + 1) * 128].rearrange(
                                "(kc pq) jj -> pq kc jj", pq=128),
                            [Bwb_], [Bwd])
                    for kc in range(NF):
                        ksl = slice(kc * 128, (kc + 1) * 128)
                        xsl = slice(kc * C, (kc + 1) * C)
                        mm([Bpp[par]], [Bwbhi[par], Bxnhi], pp[par][:],
                           wbhi[par][:, ksl], xnhi[:, xsl],
                           start=(kc == 0), stop=False)
                        mm([Bpp[par]], [Bwbhi[par], Bxnlo], pp[par][:],
                           wbhi[par][:, ksl], xnlo[:, xsl], start=False,
                           stop=False)
                        mm([Bpp[par]], [Bwblo[par], Bxnhi], pp[par][:],
                           wblo[par][:, ksl], xnhi[:, xsl], start=False,
                           stop=(kc == NF - 1))
                    op("vector", "tensor_scalar", [Bscra], [Bpp[par], Bbqc],
                       scra[:], pp[par][:], bqc[:, o:o + 1], None, OP.add)
                    split_hilo(scra[:], Bscra, qhi[:, o * C:(o + 1) * C], Bqhi,
                               qlo[:, o * C:(o + 1) * C], Bqlo)
                    op("vector", "tensor_tensor", [Bscrb], [Bscra], scrb[:],
                       scra[:], scra[:], OP.mult)
                    mm([Bpms], [Bscrb, Bones33], pms[0:33, :], ones33[:],
                       scrb[:], start=True, stop=True)
                    op("vector", "tensor_copy", [Bscrb], [Bpms], scrb[0:33, :],
                       pms[0:33, :])
                    tree(scrb, Bscrb, 33, 0, C, OP.max)
                    op("vector", "tensor_copy", [Bqm2], [Bscrb],
                       qm2[:, o:o + 1], scrb[0:33, 0:1])
                for o in range(8):
                    par = o % 2
                    for (wsrc, Bwd, wb_, Bwb_) in (
                            (wls["hi"], Bwls["hi"], wbhi[par], Bwbhi[par]),
                            (wls["lo"], Bwls["lo"], wblo[par], Bwblo[par])):
                        dma("sync", "sld",
                            wb_[:].rearrange("p (c j) -> p c j", j=128),
                            wsrc[:, o * 128:(o + 1) * 128].rearrange(
                                "(kc pq) jj -> pq kc jj", pq=128),
                            [Bwb_], [Bwd])
                    for kc in range(NF):
                        ksl = slice(kc * 128, (kc + 1) * 128)
                        xsl = slice(kc * C, (kc + 1) * C)
                        mm([Bpp[par]], [Bwbhi[par], Bxnhi], pp[par][:],
                           wbhi[par][:, ksl], xnhi[:, xsl],
                           start=(kc == 0), stop=False)
                        mm([Bpp[par]], [Bwbhi[par], Bxnlo], pp[par][:],
                           wbhi[par][:, ksl], xnlo[:, xsl], start=False,
                           stop=False)
                        mm([Bpp[par]], [Bwblo[par], Bxnhi], pp[par][:],
                           wblo[par][:, ksl], xnhi[:, xsl], start=False,
                           stop=(kc == NF - 1))
                    op("scalar", "activation", [Bscra], [Bpp[par], Bblc],
                       scra[:], pp[par][:], AF.Silu, bias=blc[:, o:o + 1],
                       scale=1.0)
                    lsl = slice(par * C, (par + 1) * C)
                    split_hilo(scra[:], Bscra, lathi[:, lsl], Blathi,
                               latlo[:, lsl], Blatlo)
                    dma("sync", "sld", kvwh[:, 0:256],
                        wks["hi"][o * 128:(o + 1) * 128, :], [Bkvwh], [Bwks["hi"]])
                    dma("sync", "sld", kvwh[:, 256:512],
                        wvs["hi"][o * 128:(o + 1) * 128, :], [Bkvwh], [Bwvs["hi"]])
                    dma("sync", "sld", kvwl[:, 0:256],
                        wks["lo"][o * 128:(o + 1) * 128, :], [Bkvwl], [Bwks["lo"]])
                    dma("sync", "sld", kvwl[:, 256:512],
                        wvs["lo"][o * 128:(o + 1) * 128, :], [Bkvwl], [Bwvs["lo"]])
                    for tgt in range(4):
                        pt = (psA[0], psA[1], psB[0], psB[1])[tgt]
                        Bpt = (BpsA[0], BpsA[1], BpsB[0], BpsB[1])[tgt]
                        wsl = slice(tgt * 128, (tgt + 1) * 128)
                        mm([Bpt], [Bkvwh, Blathi], pt[:], kvwh[:, wsl],
                           lathi[:, lsl], start=(o == 0), stop=False)
                        mm([Bpt], [Bkvwh, Blatlo], pt[:], kvwh[:, wsl],
                           latlo[:, lsl], start=False, stop=False)
                        mm([Bpt], [Bkvwl, Blathi], pt[:], kvwl[:, wsl],
                           lathi[:, lsl], start=False, stop=(o == 7))
                for o2 in range(2):
                    op("vector", "tensor_scalar", [Bscra], [BpsA[o2], Bbkc],
                       scra[:], psA[o2][:], bkc[:, o2:o2 + 1], None, OP.add)
                    split_hilo(scra[:], Bscra,
                               khi[:, o2 * T + qc * C: o2 * T + (qc + 1) * C],
                               Bkhi,
                               klo[:, o2 * T + qc * C: o2 * T + (qc + 1) * C],
                               Bklo)
                    op("vector", "tensor_tensor", [Bscrb], [Bscra], scrb[:],
                       scra[:], scra[:], OP.mult)
                    mm([Bpms], [Bscrb, Bones33], pms[0:33, :], ones33[:],
                       scrb[:], start=True, stop=True)
                    op("vector", "tensor_copy", [Bscrb], [Bpms], scrb[0:33, :],
                       pms[0:33, :])
                    tree(scrb, Bscrb, 33, 0, C, OP.max)
                    op("vector", "tensor_tensor", [Bkmax2], [Bkmax2, Bscrb],
                       kmax2[:, o2:o2 + 1], kmax2[:, o2:o2 + 1],
                       scrb[0:33, 0:1], OP.max)
                for o2 in range(2):
                    op("vector", "tensor_scalar", [Bscra], [BpsB[o2], Bbvc],
                       scra[:], psB[o2][:], bvc[:, o2:o2 + 1], None, OP.add)
                    split_hilo(scra[:], Bscra, wbf[:, 0:C], Bwbf,
                               wbf2[:, 0:C], Bwbf2)
                    for ts_ in range(4):
                        tile = qc * 4 + ts_
                        for hl in range(2):
                            src = wbf if hl == 0 else wbf2
                            Bsrc = Bwbf if hl == 0 else Bwbf2
                            dst = vhi if hl == 0 else vlo
                            Bdst = Bvhi if hl == 0 else Bvlo
                            op("vector", "tensor_copy", [Bscrb], [Bsrc],
                               scrb[:, 0:128], src[:, ts_ * 128:(ts_ + 1) * 128])
                            tr([Bpp[0]], [Bscrb, Bidt], pp[0][:, 0:128],
                               scrb[:, 0:128], idt[:])
                            op("vector", "tensor_copy", [Bdst], [Bpp[0]],
                               dst[:, tile * 256 + o2 * 128:
                                   tile * 256 + (o2 + 1) * 128],
                               pp[0][:, 0:128])
                for j in range(ATTN_NPAIR if ATTN_INNER else 0):
                    kch = j // 4
                    ga = 2 * kch
                    op("vector", "tensor_tensor", [Bbpair], [Bqm2, Bkmax2],
                       bpair[:], qm2[:, j:j + 1], kmax2[:, kch:kch + 1], OP.mult)
                    op("scalar", "activation", [Bbpair], [Bbpair], bpair[:],
                       bpair[:], AF.Sqrt)
                    op("vector", "tensor_scalar", [Bbpair], [Bbpair], bpair[:],
                       bpair[:], -0.125, None, OP.mult)
                    tr([BpsB[0]], [Bbpair, Bidt], psB[0][0:1, 0:33],
                       bpair[0:33, :], idt[0:33, 0:33])
                    op("vector", "tensor_copy", [BbnegT], [BpsB[0]], bnegT[:],
                       psB[0][0:1, 0:33])
                    mm([BpsB[0]], [BbnegT, Bonesr], psB[0][:, 0:1], onesr[:],
                       bnegT[0:1, 0:1], start=True, stop=True)
                    op("vector", "tensor_copy", [BbiasA], [BpsB[0]], biasA[:],
                       psB[0][:, 0:1])
                    mm([BpsB[0]], [BbnegT, Bonesr], psB[0][:, 0:1], onesr[:],
                       bnegT[0:1, 32:33], start=True, stop=True)
                    op("vector", "tensor_copy", [BbiasB], [BpsB[0]], biasB[:],
                       psB[0][:, 0:1])
                    nkt = 4 * qc + 4
                    for kt in range(nkt if ATTN_LVL >= 2 else 0):
                        par = kt % 2
                        off = 0 if kt < 4 * qc else (kt - 4 * qc) * 128
                        w_ = C - off
                        ktsl = slice(kch * T + kt * 128, kch * T + (kt + 1) * 128)
                        qsl = slice(j * C + off, (j + 1) * C)
                        esl = slice(0, w_)
                        mm([BpsA[par]], [Bkhi, Bqhi], psA[par][:, esl],
                           khi[0:64, ktsl], qhi[0:64, qsl], start=True, stop=False)
                        mm([BpsA[par]], [Bkhi, Bqlo], psA[par][:, esl],
                           khi[0:64, ktsl], qlo[0:64, qsl], start=False,
                           stop=False)
                        mm([BpsA[par]], [Bklo, Bqhi], psA[par][:, esl],
                           klo[0:64, ktsl], qhi[0:64, qsl], start=False,
                           stop=True)
                        mm([BpsB[par]], [Bkhi, Bqhi], psB[par][:, esl],
                           khi[64:128, ktsl], qhi[64:128, qsl], start=True,
                           stop=False)
                        mm([BpsB[par]], [Bkhi, Bqlo], psB[par][:, esl],
                           khi[64:128, ktsl], qlo[64:128, qsl], start=False,
                           stop=False)
                        mm([BpsB[par]], [Bklo, Bqhi], psB[par][:, esl],
                           klo[64:128, ktsl], qhi[64:128, qsl], start=False,
                           stop=True)
                        for hd2 in range(2 if ATTN_LVL >= 3 else 0):
                            pse = psA[par] if hd2 == 0 else psB[par]
                            Bpse = BpsA[par] if hd2 == 0 else BpsB[par]
                            bias_ = biasA if hd2 == 0 else biasB
                            Bbias_ = BbiasA if hd2 == 0 else BbiasB
                            ei = par * 2 + hd2
                            op("scalar", "activation", [Bef[ei]], [Bpse, Bbias_],
                               ef[ei][:, esl], pse[:, esl], AF.Exp,
                               bias=bias_[:], scale=0.125)
                            if kt >= 4 * qc:
                                op("vector", "tensor_tensor", [Bef[ei]],
                                   [Bef[ei], Bm0], ef[ei][:, 0:128],
                                   ef[ei][:, 0:128], m0s[:], OP.mult)
                            op("vector", "tensor_copy", [Behi[ei]], [Bef[ei]],
                               ehi[ei][:, esl], ef[ei][:, esl])
                            op("vector", "tensor_tensor", [Belo[ei]],
                               [Bef[ei], Behi[ei]], elo[ei][:, esl],
                               ef[ei][:, esl], ehi[ei][:, esl], OP.subtract)
                            if ATTN_LVL < 4:
                                continue
                            prs = pms if hd2 == 0 else pp[0]
                            Bprs = Bpms if hd2 == 0 else Bpp[0]
                            mm([Bprs], [Behi[ei], Bonescbf],
                               prs[0:1, off:C], onescbf[:],
                               ehi[ei][:, esl], start=(kt == 0), stop=False)
                            mm([Bprs], [Belo[ei], Bonescbf],
                               prs[0:1, off:C], onescbf[:],
                               elo[ei][:, esl], start=False,
                               stop=(kt == nkt - 1))
                            if ATTN_LVL < 5:
                                continue
                            g = ga + hd2
                            vt = slice(kt * 256 + g * 64, kt * 256 + (g + 1) * 64)
                            ob = hd2 * 64
                            tp = (0, ob)
                            mm([Bpctx], [Bvhi, Behi[ei]], pctx[ob:ob + 64, off:C],
                               vhi[:, vt], ehi[ei][:, esl],
                               start=(kt == 0), stop=False, tile_position=tp)
                            mm([Bpctx], [Bvhi, Belo[ei]], pctx[ob:ob + 64, off:C],
                               vhi[:, vt], elo[ei][:, esl],
                               start=False, stop=False, tile_position=tp)
                            mm([Bpctx], [Bvlo, Behi[ei]], pctx[ob:ob + 64, off:C],
                               vlo[:, vt], ehi[ei][:, esl],
                               start=False, stop=(kt == nkt - 1),
                               tile_position=tp)
                    if ATTN_LVL < 6:
                        continue
                    op("vector", "tensor_copy", [Brs0], [Bpms], rs0[:],
                       pms[0:1, :])
                    op("vector", "tensor_copy", [Brs1], [Bpp[0]], rs1[:],
                       pp[0][0:1, :])
                    op("vector", "reciprocal", [Brs0], [Brs0], rs0[:], rs0[:])
                    op("vector", "reciprocal", [Brs1], [Brs1], rs1[:], rs1[:])
                    mm([BpsA[0]], [Brs0, BmaskA], psA[0][:], maskA[:],
                       rs0[:], start=True, stop=False)
                    mm([BpsA[0]], [Brs1, BmaskB], psA[0][:], maskB[:],
                       rs1[:], start=False, stop=True)
                    op("vector", "tensor_copy", [Brbq], [BpsA[0]], rbq[:],
                       psA[0][:])
                    op("vector", "tensor_tensor", [Bscra], [Bpctx, Brbq],
                       scra[:], pctx[:], rbq[:], OP.mult)
                    split_hilo(scra[:], Bscra, ctxhi[:, j * C:(j + 1) * C],
                               Bctxhi, ctxlo[:, j * C:(j + 1) * C], Bctxlo)
                for o in range(NF if ATTN_WO else 0):
                    par = o % 2
                    for (wsrc, Bwd, wb_, Bwb_) in (
                            (woe["hi"], Bwoe["hi"], wbhi[par], Bwbhi[par]),
                            (woe["lo"], Bwoe["lo"], wblo[par], Bwblo[par])):
                        dma("sync", "sld",
                            wb_[:, 0:8 * 128].rearrange("p (c j) -> p c j", j=128),
                            wsrc[:, o * 128:(o + 1) * 128].rearrange(
                                "(kc pq) jj -> pq kc jj", pq=128),
                            [Bwb_], [Bwd])
                    for kc in range(8):
                        ksl = slice(kc * 128, (kc + 1) * 128)
                        xsl = slice(kc * C, (kc + 1) * C)
                        mm([Bpp[par]], [Bwbhi[par], Bctxhi], pp[par][:],
                           wbhi[par][:, ksl], ctxhi[:, xsl],
                           start=(kc == 0), stop=False)
                        mm([Bpp[par]], [Bwbhi[par], Bctxlo], pp[par][:],
                           wbhi[par][:, ksl], ctxlo[:, xsl], start=False,
                           stop=False)
                        mm([Bpp[par]], [Bwblo[par], Bctxhi], pp[par][:],
                           wblo[par][:, ksl], ctxhi[:, xsl], start=False,
                           stop=(kc == 7))
                    op("vector", "tensor_scalar", [Bscra], [Bpp[par], Bboc],
                       scra[:], pp[par][:], boc[:, o:o + 1], None, OP.add)
                    op("vector", "tensor_tensor", [Bxc], [Bscra, Bxc],
                       xc[:, o * C:(o + 1) * C], xc[:, o * C:(o + 1) * C],
                       scra[:], OP.add)
                for f in range(NF):
                    dma("gpsimd", "sst", xw[f * 128:(f + 1) * 128, sl],
                        xc[:, f * C:(f + 1) * C], [Bxw], [Bxc])

        if "ffn2" in PHASES:
            ffn(final=("exit" in PHASES))

        if "ffn2" not in PHASES:
            for c in range(NCH):
                for f in range(NF):
                    dma("sync", "sld", xc[:, f * C:(f + 1) * C],
                        xw[f * 128:(f + 1) * 128, c * C:(c + 1) * C], [Bxc], [Bxw])
                for f in range(NF):
                    for ts_ in range(4):
                        par2 = ts_ % 2
                        tr([Bpp[par2]], [Bxc, Bidt], pp[par2][:, 0:128],
                           xc[:, f * C + ts_ * 128: f * C + (ts_ + 1) * 128],
                           idt[:])
                        op("vector", "tensor_copy", [Bstage], [Bpp[par2]],
                           stage[:, ts_ * 128:(ts_ + 1) * 128], pp[par2][:, 0:128])
                    for ts_ in range(4):
                        dma("gpsimd", "sst",
                            yout[c * C + ts_ * 128: c * C + (ts_ + 1) * 128,
                                 f * 128:(f + 1) * 128],
                            stage[:, ts_ * 128:(ts_ + 1) * 128], [By], [Bstage])

        def _fin(e):
            for s in ("sld", "sst"):
                if p.cnt.get(s):
                    e.wait_ge(p.sems[s], p.cnt[s])
        p.raw("gpsimd", _fin)

        for s, v in p.cnt.items():
            assert v < 60000, f"semaphore {s} overflow risk: {v}"

        with nc.Block() as block:
            @block.sync
            def _(eng):
                for fn in p.ops["sync"]:
                    fn(eng)

            @block.tensor
            def _(eng):
                p._pad_src("tensor")
                for i, fn in enumerate(p.ops["tensor"]):
                    try:
                        fn(eng)
                    except Exception:
                        print("TENSOR OP FAIL at", i, p.src["tensor"][i] if i < len(p.src["tensor"]) else "?")
                        raise

            @block.vector
            def _(eng):
                for fn in p.ops["vector"]:
                    fn(eng)

            @block.scalar
            def _(eng):
                for fn in p.ops["scalar"]:
                    fn(eng)

            @block.gpsimd
            def _(eng):
                for fn in p.ops["gpsimd"]:
                    fn(eng)

    return nc


_NC_CACHE = [None]


def kernel(**inputs):
    x = np.ascontiguousarray(np.asarray(inputs["L0_attn_x"], dtype=np.float32))
    f32 = lambda a: np.ascontiguousarray(np.asarray(a, dtype=np.float32))
    base = {
        "Wq": f32(inputs["L0_attn_Wq"]),
        "bq": f32(inputs["L0_attn_bq"]).reshape(D, 1),
        "Wl": f32(inputs["L0_attn_Wl"]),
        "bl": f32(inputs["L0_attn_bl"]).reshape(1, LAT),
        "Wk": f32(inputs["L0_attn_Wk"]),
        "bk": f32(inputs["L0_attn_bk"]).reshape(1, NKV * KD),
        "Wv": f32(inputs["L0_attn_Wv"]),
        "bv": f32(inputs["L0_attn_bv"]).reshape(1, NKV * KD),
        "Wq2kv": f32(inputs["L0_attn_Wq2kv"]),
        "Wkv2h": f32(inputs["L0_attn_Wkv2h"]),
        "bkv2h": f32(inputs["L0_attn_bkv2h"]).reshape(HD, 1),
        "Wo": f32(inputs["L0_attn_Wo"]),
        "bo": f32(inputs["L0_attn_bo"]).reshape(1, D),
        "Wf": f32(inputs["L0_ffn_weight"]),
        "gam": f32(inputs["L0_ffn_norm_weight"]).reshape(D, 1),
        "ident": np.eye(128, dtype=np.float32),
        "m0": np.triu(np.ones((128, 128), dtype=np.float32)),
    }
    if _NC_CACHE[0] is None:
        _NC_CACHE[0] = build_nc()
    nc = _NC_CACHE[0]
    in_maps = [dict(base, x_in=x[c]) for c in range(8)]
    res = run_bass_kernel_spmd(nc, in_maps, core_ids=list(range(8)))
    if DEBUG:
        kernel.dbg = [np.asarray(r["dbg"], dtype=np.float32) for r in res.results]
    return np.stack([np.asarray(r["y"], dtype=np.float32) for r in res.results])


# version 06
# speedup vs baseline: 1.1719x; 1.1455x over previous
import contextlib
import numpy as np
import concourse.bass as bass
import concourse.mybir as mybir
from concourse.bass_utils import run_bass_kernel_spmd

F32 = mybir.dt.float32
I32 = mybir.dt.int32
BF16 = mybir.dt.bfloat16
OP = mybir.AluOpType
AF = mybir.ActivationFunctionType
AX = mybir.AxisListType

T = 2048
D = 2048
C = 512
NCH = T // C
NF = D // 128
LAT = 1024
NH, NKV, HD, KD = 16, 4, 128, 64
NPAIR = 8
PAIR_A = [0, 1, 2, 3, 8, 9, 10, 11]

PHASES = ("entry", "prep", "ffn1", "attn", "ffn2", "exit")
DEBUG = False
ATTN_NCH = NCH
ATTN_NPAIR = NPAIR
ATTN_INNER = True
ATTN_LVL = 6
ATTN_WO = True

ENGS = ("sync", "tensor", "vector", "scalar", "gpsimd")
ESEM = {"sync": None, "tensor": "smm", "vector": "sdv", "scalar": "sac",
        "gpsimd": "spl"}


class Buf:
    __slots__ = ("w", "r", "wvi")

    def __init__(self):
        self.w = None
        self.r = []
        self.wvi = -10


class B:

    def __init__(self):
        self.ops = {e: [] for e in ENGS}
        self.src = {e: [] for e in ENGS}
        self.cnt = {}
        self.sems = {}
        self.last_vec = None
        self.sep_ap = None
        self.vec_idx = 0
        self.waited = {e: {} for e in ENGS}

    def _bump(self, s, n):
        self.cnt[s] = self.cnt.get(s, 0) + n
        return self.cnt[s]

    def _pad_src(self, eng):
        pass

    def _waits(self, eng, ins, outs):
        evs = []
        for b in ins:
            if b.w:
                evs.append(b.w)
        for b in outs:
            if b.w:
                evs.append(b.w)
            evs.extend(b.r)
        best = {}
        for s, v, src in evs:
            if src == eng and src is not None:
                continue
            if s not in best or best[s] < v:
                best[s] = v
        for s, v in best.items():
            if self.waited[eng].get(s, -1) >= v:
                continue
            self.waited[eng][s] = v
            self.src[eng].append("wait")
            self.ops[eng].append(lambda e, sm=s, vv=v: e.wait_ge(self.sems[sm], vv))

    def op(self, eng, meth, outs, ins, *args, **kw):
        if eng == "vector" and self.sep_ap is not None and meth != "_sep":
            need = 0
            for b in ins:
                gap = self.vec_idx - b.wvi
                if 0 < gap < 3:
                    need = max(need, 3 - gap)
            for _ in range(need):
                self.op("vector", "_sep", [], [])
        if meth == "_sep":
            eng, meth, args, kw = "vector", "tensor_copy",                 (self.sep_ap, self.sep_ap), {}
        if eng == "vector":
            self.last_vec = meth
        self._waits(eng, ins, outs)
        s = ESEM[eng]
        v = self._bump(s, 1)
        import sys as _s
        fr = _s._getframe(1)
        if fr.f_code.co_name in ("mm", "tr", "op"):
            fr = _s._getframe(2)
        self.src[eng].append(f"{fr.f_lineno}:{meth}")
        self.ops[eng].append(
            lambda e, m=meth, a=args, k2=dict(kw), sm=s:
            getattr(e, m)(*a, **k2).then_inc(self.sems[sm], 1))
        ev = (s, v, eng)
        for b in outs:
            b.w = ev
            b.r = []
            if eng == "vector":
                b.wvi = self.vec_idx
        for b in ins:
            b.r.append(ev)
        if eng == "vector":
            self.vec_idx += 1
        return ev

    def mm(self, outs, ins, *args, **kw):
        kw.setdefault("skip_group_check", True)
        return self.op("tensor", "matmul", outs, ins, *args, **kw)

    def tr(self, outs, ins, *args, **kw):
        return self.op("tensor", "transpose", outs, ins, *args, **kw)

    def dma(self, eng, sname, dst_ap, src_ap, outs, ins, **kw):
        self._waits(eng, ins, outs)
        v = self._bump(sname, 16)
        self.ops[eng].append(
            lambda e, o=dst_ap, i=src_ap, sm=sname, k2=dict(kw):
            e.dma_start(out=o, in_=i, **k2).then_inc(self.sems[sm], 16))
        ev = (sname, v, None)
        for b in outs:
            b.w = ev
            b.r = []
        for b in ins:
            b.r.append(ev)
        return ev

    def collective(self, wouts, wins, *args, **kw):
        self._waits("gpsimd", wins, wouts)
        v = self._bump("scc", 1)
        self.ops["gpsimd"].append(
            lambda e, a=args, k2=dict(kw):
            e.collective_compute(*a, **k2).then_inc(self.sems["scc"]))
        ev = ("scc", v, None)
        for b in wouts:
            b.w = ev
            b.r = []
        for b in wins:
            b.r.append(ev)
        return ev

    def raw(self, eng, fn):
        self.ops[eng].append(fn)


def build_nc():
    nc = bass.Bass("TRN2", target_bir_lowering=False,
                   detect_race_conditions=False)
    p = B()

    xin = nc.dram_tensor("x_in", [T, D], F32, kind="ExternalInput")
    hspec = [
        ("Wq", [D, D]), ("bq", [D, 1]), ("Wl", [D, LAT]), ("bl", [1, LAT]),
        ("Wk", [LAT, NKV * KD]), ("bk", [1, NKV * KD]),
        ("Wv", [LAT, NKV * KD]), ("bv", [1, NKV * KD]),
        ("Wq2kv", [HD, KD]), ("Wkv2h", [KD, HD]), ("bkv2h", [HD, 1]),
        ("Wo", [D, D]), ("bo", [1, D]),
        ("Wf", [D, D]), ("gam", [D, 1]),
        ("ident", [128, 128]), ("m0", [128, 128]),
    ]
    h = {name: nc.dram_tensor(name, shape, F32, kind="ExternalInput")
         for name, shape in hspec}
    yout = nc.dram_tensor("y", [T, D], F32, kind="ExternalOutput")
    dbg = nc.dram_tensor("dbg", [128, 1600], F32, kind="ExternalOutput") if DEBUG else None

    xw = nc.dram_tensor("x_work", [D, T], F32)
    wqTs = nc.dram_tensor("wqT_s", [D, D], BF16)
    wqe = {s: nc.dram_tensor(f"wqe_{s}", [D, NH * KD], BF16) for s in ("hi", "lo")}
    wls = {s: nc.dram_tensor(f"wl_{s}", [D, LAT], BF16) for s in ("hi", "lo")}
    wks = {s: nc.dram_tensor(f"wk_{s}", [LAT, NKV * KD], BF16) for s in ("hi", "lo")}
    wvs = {s: nc.dram_tensor(f"wv_{s}", [LAT, NKV * KD], BF16) for s in ("hi", "lo")}
    woe = {s: nc.dram_tensor(f"woe_{s}", [NH * KD, D], BF16) for s in ("hi", "lo")}
    cc_in = nc.dram_tensor("cc_in", [1, 8], F32)
    cc_out = nc.dram_tensor("cc_out", [1, 8], F32)
    scal_h = nc.dram_tensor("scal_h", [1, 8], F32)
    row_h = nc.dram_tensor("row_h", [1, D], F32)
    rows_h = nc.dram_tensor("rows_h", [2, T], F32)

    es = contextlib.ExitStack()
    with es:
        def sb(name, shape, dt=F32):
            t = es.enter_context(nc.sbuf_tensor(name, shape, dt))
            return t, Buf()

        def psm(name, shape, dt=F32):
            t = es.enter_context(nc.psum_tensor(name, shape, dt))
            return t, Buf()

        for s in ("sld", "sst", "smm", "sdv", "sac", "spl", "scc"):
            p.sems[s] = es.enter_context(nc.semaphore(s))

        xc, Bxc = sb("xc", [128, NF * C])
        abfA, BabfA = sb("abfA", [128, NF * C], BF16)
        abfB, BabfB = sb("abfB", [128, NF * C], BF16)
        qhi, Bqhi = sb("qhi", [128, NPAIR * C], BF16)
        qlo, Bqlo = sb("qlo", [128, NPAIR * C], BF16)
        lathi, Blathi = sb("lathi", [128, 2 * C], BF16)
        latlo, Blatlo = sb("latlo", [128, 2 * C], BF16)
        khi, Bkhi = sb("khi", [128, 2 * T], BF16)
        klo, Bklo = sb("klo", [128, 2 * T], BF16)
        vhi, Bvhi = sb("vhi", [128, (T // 128) * 256], BF16)
        vlo, Bvlo = sb("vlo", [128, (T // 128) * 256], BF16)
        af32, Baf32 = sb("af32", [128, 4 * C])
        ehi, Behi = zip(*[sb(f"ehi{i}", [128, C], BF16) for i in range(4)])
        elo, Belo = zip(*[sb(f"elo{i}", [128, C], BF16) for i in range(4)])
        wrow, Bwrow = sb("wrow", [128, D])
        wbf, Bwbf = sb("wbf", [128, D], BF16)
        wbf2, Bwbf2 = sb("wbf2", [128, D], BF16)
        wbhi, Bwbhi = zip(*[sb(f"wbhi{i}", [128, NF * 128], BF16) for i in range(2)])
        wblo, Bwblo = zip(*[sb(f"wblo{i}", [128, NF * 128], BF16) for i in range(2)])
        kvwh, Bkvwh = sb("kvwh", [128, 4 * 128], BF16)
        kvwl, Bkvwl = sb("kvwl", [128, 4 * 128], BF16)

        t1, Bt1 = sb("t1", [128, C])
        t3, Bt3 = sb("t3", [128, C])
        scri, Bscri = sb("scri", [128, C], I32)
        scra, Bscra = sb("scra", [128, C])
        scrb, Bscrb = sb("scrb", [128, C])
        scrc, Bscrc = sb("scrc", [128, C])
        mb, Bmb = sb("mb", [128, C])
        rb, Brb = sb("rb", [128, C])
        rs0, Brs0 = sb("rs0", [1, C])
        rs1, Brs1 = sb("rs1", [1, C])
        stage, Bstage = scrc, Bscrc
        m_r, Bm_r = sb("m_r", [1, C])
        r_r, Br_r = sb("r_r", [1, C])
        v_r, Bv_r = rs1[0:1, :], Brs1
        s_r, Bs_r = rs0[0:1, :], Brs0
        brow, Bbrow = rs0[0:1, :], Brs0
        idt, Bidt = sb("idt", [128, 128])
        m0s, Bm0 = sb("m0s", [128, 128])
        onesc, Bonesc = sb("onesc", [128, 1])
        onescbf, Bonescbf = sb("onescbf", [128, 1], BF16)
        onesr, Bonesr = sb("onesr", [1, 128])
        ones33, Bones33 = sb("ones33", [128, 33])
        maskA, BmaskA = sb("maskA", [1, 128])
        maskB, BmaskB = sb("maskB", [1, 128])
        gcol, Bgcol = sb("gcol", [128, NF])
        q2kvh, Bq2kvh = sb("q2kvh", [128, KD], BF16)
        q2kvl, Bq2kvl = sb("q2kvl", [128, KD], BF16)
        kv2h_h, Bkv2h_h = sb("kv2h_h", [128, KD], BF16)
        kv2h_l, Bkv2h_l = sb("kv2h_l", [128, KD], BF16)
        bqcolh, Bbqcolh = sb("bqcolh", [128, 1], BF16)
        bqcoll, Bbqcoll = sb("bqcoll", [128, 1], BF16)
        bqc, Bbqc = sb("bqc", [128, 8])
        blc, Bblc = sb("blc", [128, 8])
        bkc, Bbkc = sb("bkc", [128, 2])
        bvc, Bbvc = sb("bvc", [128, 2])
        boc, Bboc = sb("boc", [128, NF])
        am, Bam = sb("am", [128, 1])
        s8, Bs8 = sb("s8", [1, 8])
        sc1, Bsc1 = sb("sc1", [1, 1])
        sc2, Bsc2 = sb("sc2", [1, 1])
        wsc, Bwsc = sb("wsc", [1, 1])
        rho, Brho = sb("rho", [128, 1])
        ahp, Bahp = sb("ahp", [128, 1])
        rwp, Brwp = sb("rwp", [128, 1])
        wscp, Bwscp = sb("wscp", [128, 1])
        mxc, Bmxc = sb("mxc", [128, 1])
        scp, Bscp = sb("scp", [128, 1])
        tmpc, Btmpc = sb("tmpc", [128, 1])
        sep, _Bsep = sb("sep", [128, 1])
        kmax2, Bkmax2 = sb("kmax2", [33, 2])
        qm2, Bqm2 = sb("qm2", [33, 8])
        bpair, Bbpair = sb("bpair", [33, 1])
        bnegT, BbnegT = sb("bnegT", [1, 33])
        biasA, BbiasA = sb("biasA", [128, 1])
        biasB, BbiasB = sb("biasB", [128, 1])

        rnd, Brnd = abfA, BabfA
        xnhi, Bxnhi = abfA, BabfA
        xnlo, Bxnlo = abfB, BabfB
        ctxhi = abfB[:, 0:NPAIR * C]
        ctxlo = abfB[:, NPAIR * C:2 * NPAIR * C]
        Bctxhi = Bctxlo = BabfB
        xtile, Bxtile = wrow, Bwrow
        wrow2 = af32
        Bwrow2 = Baf32
        efb = [sb(f"efb{i}", [128, C]) for i in range(4)]
        ef = [t for t, _ in efb]
        Bef = [b for _, b in efb]
        obf = wbhi[0][0:64, :]
        obf2 = wblo[0][0:64, :]
        Bobf, Bobf2 = Bwbhi[0], Bwblo[0]
        rbq, Brbq = mb, Bmb

        pp, Bpp = zip(*[psm(f"pp{i}", [128, C]) for i in range(2)])
        psA, BpsA = zip(*[psm(f"psA{i}", [128, C]) for i in range(2)])
        psB, BpsB = zip(*[psm(f"psB{i}", [128, C]) for i in range(2)])
        pctx, Bpctx = psm("pctx", [128, C])
        pms, Bpms = psm("pms", [128, C])

        BxcS = [Buf() for _ in range(NF)]
        BaS = [Buf() for _ in range(NF)]
        BbS = [Buf() for _ in range(NF)]
        BqS = [Buf() for _ in range(NPAIR)]
        BqloS = [Buf() for _ in range(NPAIR)]
        BlatS = [Buf() for _ in range(2)]
        BlatloS = [Buf() for _ in range(2)]

        Bscal = [Buf() for _ in range(8)]
        Bxw = Buf()
        BwqTs = Buf()
        Bwqe = {s: Buf() for s in ("hi", "lo")}
        Bwls = {s: Buf() for s in ("hi", "lo")}
        Bwks = {s: Buf() for s in ("hi", "lo")}
        Bwvs = {s: Buf() for s in ("hi", "lo")}
        Bwoe = {s: Buf() for s in ("hi", "lo")}
        Bccin, Bccout = Buf(), Buf()
        Bxin, By = Buf(), Buf()
        Browh = Buf()
        Browsh = Buf()
        Bh = {k: Buf() for k in h}

        p.sep_ap = sep[:]
        op = p.op
        mm = p.mm
        tr = p.tr
        dma = p.dma

        def split_hilo(src, Bsrc, dsthi, Bdh, dstlo, Bdl):
            op("vector", "tensor_copy", [Bdh], [Bsrc], dsthi, src)
            op("vector", "tensor_tensor", [Bdl], [Bsrc, Bdh], dstlo, src, dsthi,
               OP.subtract)

        def round_bf16(src, Bsrc, dst, Bdst, width):
            si = scri[:, 0:width]
            sa = scra[:, 0:width]
            sb_ = scrb[:, 0:width]
            op("vector", "tensor_copy", [Bscri], [Bsrc], si, src)
            op("vector", "scalar_tensor_tensor", [Bscra], [Bsrc, Bscri], sa, src,
               0.5, si, OP.add, OP.subtract)
            op("vector", "tensor_scalar", [Bscrb], [Bscra], sb_, sa, 1.0, None,
               OP.is_gt)
            op("vector", "scalar_tensor_tensor", [Bscra], [Bscra, Bscrb], sa, sa,
               0.0, sb_, OP.is_lt, OP.subtract)
            op("vector", "tensor_tensor", [Bdst], [Bscri, Bscra], dst, si, sa,
               OP.subtract)

        def tree(t, Bt, p_, col0, w, opk):
            while w > 1:
                w //= 2
                op("vector", "tensor_tensor", [Bt], [Bt],
                   t[0:p_, col0:col0 + w], t[0:p_, col0:col0 + w],
                   t[0:p_, col0 + w:col0 + 2 * w], opk)

        def bcast_scal(src_ap, Bsrc, dst, Bdst, lane):
            dma("gpsimd", "sst", scal_h[0:1, lane:lane + 1], src_ap,
                [Bscal[lane]], [Bsrc])
            dma("gpsimd", "sst", dst,
                scal_h[0:1, lane:lane + 1].broadcast_to((128, 1)),
                [Bdst], [Bscal[lane]])

        def cols_from_rowh(n, dstcols, Bdst):
            dma("gpsimd", "sst", dstcols,
                row_h[0:1, 0:n * 128].rearrange("a (f q) -> (a q) f", q=128),
                [Bdst], [Browh], allow_slow_non_contiguous=True)

        def bcast_row(src_ap, Bsrc, dst, Bdst):
            mm([Bpms], [Bsrc, Bonesr], pms[:], onesr[:], src_ap, start=True,
               stop=True)
            op("vector", "tensor_copy", [Bdst], [Bpms], dst, pms[:])

        op("vector", "memset", [], [], sep[:], 0.0)
        op("vector", "memset", [Bonesc], [], onesc[:], 1.0)
        op("vector", "memset", [Bonesr], [], onesr[:], 1.0)
        op("vector", "memset", [Bonescbf], [], onescbf[:], 1.0)
        op("vector", "memset", [Bones33], [], ones33[:], 0.0)
        op("vector", "memset", [Bones33], [], ones33[0:64, 0:1], 1.0)
        op("vector", "memset", [Bones33], [], ones33[64:128, 32:33], 1.0)
        op("vector", "memset", [BmaskA], [], maskA[:, 0:64], 1.0)
        op("vector", "memset", [BmaskA], [], maskA[:, 64:128], 0.0)
        op("vector", "memset", [BmaskB], [], maskB[:, 0:64], 0.0)
        op("vector", "memset", [BmaskB], [], maskB[:, 64:128], 1.0)
        dma("sync", "sld", idt[:], h["ident"].ap(), [Bidt], [])
        dma("sync", "sld", m0s[:], h["m0"].ap(), [Bm0], [])
        for f in range(NF):
            dma("sync", "sld", gcol[:, f:f + 1], h["gam"][f * 128:(f + 1) * 128, 0:1],
                [Bgcol], [])

        if "entry" in PHASES:
            for tt in range(T // 128):
                dma("sync", "sld", xtile[:], xin[tt * 128:(tt + 1) * 128, :],
                    [Bxtile], [Bxin])
                for g in range(4):
                    Bp = Bpp[g % 2]
                    for j in range(4):
                        f = g * 4 + j
                        tr([Bp], [Bxtile, Bidt], pp[g % 2][:, j * 128:(j + 1) * 128],
                           xtile[:, f * 128:(f + 1) * 128], idt[:])
                    op("vector", "tensor_copy", [Bstage], [Bp], stage[:], pp[g % 2][:])
                    for j in range(4):
                        f = g * 4 + j
                        dma("gpsimd", "sst",
                            xw[f * 128:(f + 1) * 128, tt * 128:(tt + 1) * 128],
                            stage[:, j * 128:(j + 1) * 128], [Bxw], [Bstage])

        if "prep" in PHASES:
            dma("sync", "sld", wrow[:, 0:KD], h["Wq2kv"].ap(), [Bwrow], [Bh["Wq2kv"]])
            split_hilo(wrow[:, 0:KD], Bwrow, q2kvh[:], Bq2kvh, q2kvl[:], Bq2kvl)
            dma("sync", "sld", wrow[0:KD, 128:256], h["Wkv2h"].ap(), [Bwrow],
                [Bh["Wkv2h"]])
            tr([Bpp[0]], [Bwrow, Bidt], pp[0][:, 0:KD], wrow[0:KD, 128:256],
               idt[0:KD, 0:KD])
            split_hilo(pp[0][:, 0:KD], Bpp[0], kv2h_h[:], Bkv2h_h, kv2h_l[:],
                       Bkv2h_l)

            accw = xc
            for o in range(NF):
                dma("sync", "sld", wrow[:], h["Wf"][o * 128:(o + 1) * 128, :],
                    [Bwrow], [Bh["Wf"]])
                op("scalar", "activation", [Bscrc], [Bwrow], scrc[:],
                   wrow[:, 0:C], AF.Abs)
                op("scalar", "activation", [Bscra], [Bwrow], scra[:],
                   wrow[:, C:2 * C], AF.Abs)
                op("scalar", "activation", [Bscrb], [Bwrow], scrb[:],
                   wrow[:, 2 * C:3 * C], AF.Abs)
                op("scalar", "activation", [Bt3], [Bwrow], t3[:],
                   wrow[:, 3 * C:4 * C], AF.Abs)
                for (buf, Bbuf, q4) in ((scrc, Bscrc, 0), (scra, Bscra, 1),
                                        (scrb, Bscrb, 2), (t3, Bt3, 3)):
                    if o == 0:
                        op("vector", "tensor_copy", [BxcS[q4]], [Bbuf],
                           accw[:, q4 * C:(q4 + 1) * C], buf[:])
                    else:
                        op("vector", "tensor_tensor", [BxcS[q4]],
                           [BxcS[q4], Bbuf], accw[:, q4 * C:(q4 + 1) * C],
                           accw[:, q4 * C:(q4 + 1) * C], buf[:], OP.add)
            op("vector", "tensor_tensor", [BxcS[0], BxcS[1]],
               [BxcS[2], BxcS[3]], accw[:, 0:2 * C],
               accw[:, 0:2 * C], accw[:, 2 * C:4 * C], OP.add)
            op("vector", "tensor_tensor", [BxcS[0]], [BxcS[0], BxcS[1]],
               accw[:, 0:C], accw[:, 0:C], accw[:, C:2 * C], OP.add)
            tree(accw, BxcS[0], 128, 0, C, OP.add)
            dma("gpsimd", "sst", row_h[0:1, 0:128], accw[:, 0:1], [Browh],
                [BxcS[0]])
            dma("gpsimd", "sst", scrc[:, 0:128],
                row_h[0:1, 0:128].broadcast_to((128, 128)), [Bscrc], [Browh])
            tree(scrc, Bscrc, 128, 0, 128, OP.add)
            op("vector", "tensor_scalar", [Bwscp], [Bscrc], wscp[:],
               scrc[:, 0:1], 1.0 / (D * D), 1e-5, OP.mult, OP.max)
            op("vector", "reciprocal", [Brwp], [Bwscp], rwp[:], wscp[:])
            if DEBUG:
                Bdbg = Buf()
                dma("gpsimd", "sst", dbg[:, 1570:1571], am[:], [Bdbg], [Bam],
                    allow_slow_non_contiguous=True)
                dma("gpsimd", "sst", dbg[:, 1571:1572], wscp[:], [Bdbg], [Bwscp],
                    allow_slow_non_contiguous=True)
                dma("gpsimd", "sst", dbg[:, 1573:1574], rwp[:], [Bdbg], [Brwp],
                    allow_slow_non_contiguous=True)

            for o in range(NF):
                dma("sync", "sld", wrow[:], h["Wf"][o * 128:(o + 1) * 128, :],
                    [Bwrow], [Bh["Wf"]])
                for q4 in range(4):
                    sl = slice(q4 * C, (q4 + 1) * C)
                    op("vector", "tensor_scalar", [Bwrow2], [Bwrow, Brwp],
                       wrow2[:, 0:C], wrow[:, sl], rwp[:], None, OP.mult)
                    op("vector", "tensor_scalar", [Bwrow2], [Bwrow2], wrow2[:, 0:C],
                       wrow2[:, 0:C], -1.0, 1.0, OP.max, OP.min)
                    round_bf16(wrow2[:, 0:C], Bwrow2, wbf[:, sl], Bwbf, C)
                for g in range(4):
                    Bp = Bpp[g % 2]
                    for j in range(4):
                        f = g * 4 + j
                        op("vector", "tensor_copy", [Bscrc], [Bwbf],
                           scrc[:, j * 128:(j + 1) * 128],
                           wbf[:, f * 128:(f + 1) * 128])
                    for j in range(4):
                        tr([Bp], [Bscrc, Bidt], pp[g % 2][:, j * 128:(j + 1) * 128],
                           scrc[:, j * 128:(j + 1) * 128], idt[:])
                    op("vector", "tensor_copy", [Bwbf2], [Bp],
                       wbf2[:, g * C:(g + 1) * C], pp[g % 2][:])
                for f in range(NF):
                    dma("gpsimd", "sst",
                        wqTs[f * 128:(f + 1) * 128, o * 128:(o + 1) * 128],
                        wbf2[:, f * 128:(f + 1) * 128], [BwqTs], [Bwbf2])

            for j in range(NPAIR):
                for ab in range(2):
                    hh = PAIR_A[j] + 4 * ab
                    dst = j * 128 + ab * 64
                    dma("sync", "sld",
                        wrow[:].rearrange("p (c j) -> p c j", j=128),
                        h["Wq"][:, hh * 128:(hh + 1) * 128].rearrange(
                            "(c pq) jj -> pq c jj", pq=128),
                        [Bwrow], [Bh["Wq"]])
                    for g in range(4):
                        Bp = Bpp[g % 2]
                        for jj in range(4):
                            f = g * 4 + jj
                            tr([Bp], [Bwrow, Bidt],
                               pp[g % 2][:, jj * 128:(jj + 1) * 128],
                               wrow[:, f * 128:(f + 1) * 128], idt[:])
                        split_hilo(pp[g % 2][:], Bp, wbf[:, g * C:(g + 1) * C],
                                   Bwbf, wbf2[:, g * C:(g + 1) * C], Bwbf2)
                    for half in range(2):
                        Bp = Bpp[half]
                        for cc in range(8):
                            c = half * 8 + cc
                            bsl = slice(c * 128, (c + 1) * 128)
                            osl = slice(cc * KD, (cc + 1) * KD)
                            mm([Bp], [Bwbf, Bq2kvh], pp[half][:, osl],
                               wbf[:, bsl], q2kvh[:], start=True, stop=False)
                            mm([Bp], [Bwbf, Bq2kvl], pp[half][:, osl],
                               wbf[:, bsl], q2kvl[:], start=False, stop=False)
                            mm([Bp], [Bwbf2, Bq2kvh], pp[half][:, osl],
                               wbf2[:, bsl], q2kvh[:], start=False, stop=True)
                        split_hilo(pp[half][:], Bp, wbf[:, half * C:(half + 1) * C],
                                   Bwbf, wbf2[:, half * C:(half + 1) * C], Bwbf2)
                        for cc in range(8):
                            c = half * 8 + cc
                            dma("gpsimd", "sst",
                                wqe["hi"][c * 128:(c + 1) * 128, dst:dst + KD],
                                wbf[:, half * C + cc * KD: half * C + (cc + 1) * KD],
                                [Bwqe["hi"]], [Bwbf])
                            dma("gpsimd", "sst",
                                wqe["lo"][c * 128:(c + 1) * 128, dst:dst + KD],
                                wbf2[:, half * C + cc * KD: half * C + (cc + 1) * KD],
                                [Bwqe["lo"]], [Bwbf2])

            dma("sync", "sld", wrow[0:1, 0:128] if False else wrow[0:128, 0:1],
                h["bkv2h"].ap(), [Bwrow], [Bh["bkv2h"]])
            op("vector", "tensor_copy", [Bscrc], [Bwrow], scrc[:, 0:1],
               wrow[0:128, 0:1])
            nh_i = 0
            for j in range(NPAIR):
                for ab in range(2):
                    hh = PAIR_A[j] + 4 * ab
                    dst = j * 128 + ab * 64
                    dma("sync", "sld", wrow[:], h["Wo"][hh * 128:(hh + 1) * 128, :],
                        [Bwrow], [Bh["Wo"]])
                    split_hilo(wrow[:], Bwrow, wbf[:], Bwbf, wbf2[:], Bwbf2)
                    for cc in range(4):
                        sl = slice(cc * C, (cc + 1) * C)
                        Bp = Bpp[cc % 2]
                        mm([Bp], [Bkv2h_h, Bwbf], pp[cc % 2][0:KD, :],
                           kv2h_h[:], wbf[:, sl], start=True, stop=False)
                        mm([Bp], [Bkv2h_l, Bwbf], pp[cc % 2][0:KD, :],
                           kv2h_l[:], wbf[:, sl], start=False, stop=False)
                        mm([Bp], [Bkv2h_h, Bwbf2], pp[cc % 2][0:KD, :],
                           kv2h_h[:], wbf2[:, sl], start=False, stop=True)
                        mm([Bpms], [Bscrc, Bwrow], pms[32 * cc:32 * cc + 1, :],
                           scrc[:, 0:1], wrow[:, sl],
                           start=(nh_i == 0), stop=(nh_i == NH - 1),
                           tile_position=(0, 32 * cc))
                        split_hilo(pp[cc % 2][0:KD, :], Bp, obf[:, sl], Bobf,
                                   obf2[:, sl], Bobf2)
                    dma("gpsimd", "sst", woe["hi"][dst:dst + KD, :], obf[:],
                        [Bwoe["hi"]], [Bobf])
                    dma("gpsimd", "sst", woe["lo"][dst:dst + KD, :], obf2[:],
                        [Bwoe["lo"]], [Bobf2])
                    nh_i += 1
            for cc in range(4):
                dma("sync", "sld", brow[:], h["bo"][0:1, cc * C:(cc + 1) * C],
                    [Bbrow], [Bh["bo"]])
                op("vector", "tensor_tensor", [Bbrow], [Bpms, Bbrow], brow[:],
                   pms[32 * cc:32 * cc + 1, :], brow[:], OP.add)
                dma("gpsimd", "sst", row_h[0:1, cc * C:(cc + 1) * C], brow[:],
                    [Browh], [Bbrow])
            cols_from_rowh(NF, boc[:], Bboc)

            for j in range(NPAIR):
                for ab in range(2):
                    hh = PAIR_A[j] + 4 * ab
                    dma("sync", "sld", wrow[0:128, 0:1],
                        h["bq"][hh * 128:(hh + 1) * 128, 0:1], [Bwrow], [Bh["bq"]])
                    split_hilo(wrow[0:128, 0:1], Bwrow, bqcolh[:], Bbqcolh,
                               bqcoll[:], Bbqcoll)
                    mm([Bpms], [Bbqcolh, Bq2kvh], pms[0:1, 0:KD], bqcolh[:],
                       q2kvh[:], start=True, stop=False)
                    mm([Bpms], [Bbqcolh, Bq2kvl], pms[0:1, 0:KD], bqcolh[:],
                       q2kvl[:], start=False, stop=False)
                    mm([Bpms], [Bbqcoll, Bq2kvh], pms[0:1, 0:KD], bqcoll[:],
                       q2kvh[:], start=False, stop=True)
                    op("vector", "tensor_copy", [Bbrow], [Bpms], brow[0:1, 0:KD],
                       pms[0:1, 0:KD])
                    dma("gpsimd", "sst",
                        row_h[0:1, j * 128 + ab * 64: j * 128 + ab * 64 + KD],
                        brow[0:1, 0:KD], [Browh], [Bbrow])
            cols_from_rowh(8, bqc[:], Bbqc)

            for (src, Bsrc_, n, dstc, Bdstc) in (
                    (h["bl"], Bh["bl"], 8, blc, Bblc),
                    (h["bk"], Bh["bk"], 2, bkc, Bbkc),
                    (h["bv"], Bh["bv"], 2, bvc, Bbvc)):
                dma("gpsimd", "sst", row_h[0:1, 0:n * 128], src[0:1, 0:n * 128],
                    [Browh], [Bsrc_])
                cols_from_rowh(n, dstc[:], Bdstc)

            for o in range(NF):
                dma("sync", "sld", wrow[:, 0:LAT], h["Wl"][o * 128:(o + 1) * 128, :],
                    [Bwrow], [Bh["Wl"]])
                split_hilo(wrow[:, 0:LAT], Bwrow, wbf[:, 0:LAT], Bwbf,
                           wbf2[:, 0:LAT], Bwbf2)
                dma("gpsimd", "sst", wls["hi"][o * 128:(o + 1) * 128, :],
                    wbf[:, 0:LAT], [Bwls["hi"]], [Bwbf])
                dma("gpsimd", "sst", wls["lo"][o * 128:(o + 1) * 128, :],
                    wbf2[:, 0:LAT], [Bwls["lo"]], [Bwbf2])
            for (mat, Bmat, dsts, Bd) in ((h["Wk"], Bh["Wk"], wks, Bwks),
                                          (h["Wv"], Bh["Wv"], wvs, Bwvs)):
                for o in range(8):
                    dma("sync", "sld", wrow[:, 0:256], mat[o * 128:(o + 1) * 128, :],
                        [Bwrow], [Bmat])
                    split_hilo(wrow[:, 0:256], Bwrow, wbf[:, 0:256], Bwbf,
                               wbf2[:, 0:256], Bwbf2)
                    dma("gpsimd", "sst", dsts["hi"][o * 128:(o + 1) * 128, :],
                        wbf[:, 0:256], [Bd["hi"]], [Bwbf])
                    dma("gpsimd", "sst", dsts["lo"][o * 128:(o + 1) * 128, :],
                        wbf2[:, 0:256], [Bd["lo"]], [Bwbf2])

        def ln_chunk_stats(extra_rms):
            for f in range(NF):
                op("scalar", "activation", [Bscra], [BxcS[f]], scra[:],
                   xc[:, f * C:(f + 1) * C], AF.Square)
                mm([Bpms], [BxcS[f], Bonesc], pms[0:1, :], onesc[:],
                   xc[:, f * C:(f + 1) * C], start=(f == 0), stop=(f == NF - 1))
                mm([Bpctx], [Bscra, Bonesc], pctx[0:1, :], onesc[:], scra[:],
                   start=(f == 0), stop=(f == NF - 1))
            op("vector", "tensor_scalar", [Bm_r], [Bpms], m_r[:], pms[0:1, :],
               1.0 / D, None, OP.mult)
            op("vector", "tensor_scalar", [Bv_r], [Bpctx], v_r[:], pctx[0:1, :],
               1.0 / D, None, OP.mult)
            op("vector", "tensor_tensor", [Bs_r], [Bm_r], s_r[:], m_r[:], m_r[:],
               OP.mult)
            op("vector", "tensor_tensor", [Bv_r], [Bv_r, Bs_r], v_r[:], v_r[:],
               s_r[:], OP.subtract)
            if extra_rms:
                op("vector", "tensor_scalar", [Bv_r], [Bv_r], v_r[:], v_r[:],
                   1.0 + 1e-6, 1e-11, OP.mult, OP.add)
            else:
                op("vector", "tensor_scalar", [Bv_r], [Bv_r], v_r[:], v_r[:],
                   1e-5, None, OP.add)
            op("vector", "reciprocal", [Bs_r], [Bv_r], s_r[:], v_r[:])
            op("scalar", "activation", [Br_r], [Bs_r], r_r[:], s_r[:], AF.Sqrt)

        ffn_calls = [0]

        def ffn_dbg(c, o):
            if not DEBUG or ffn_calls[0] != 1 or c != 0 or o != 0:
                return
            Bdbg = Buf()
            op("vector", "tensor_copy", [Bscrb], [BaS[0]], scrb[:], rnd[:, 0:C])
            dma("gpsimd", "sst", dbg[:, 0:C], scrb[:], [Bdbg], [Bscrb])
            op("vector", "tensor_copy", [Bscrb], [Bwbhi[0]], scrb[:],
               wbhi[0][:, 0:C])
            dma("gpsimd", "sst", dbg[:, C:2 * C], scrb[:], [Bdbg], [Bscrb])
            dma("gpsimd", "sst", dbg[:, 2 * C:3 * C], xc[:, 0:C], [Bdbg], [BxcS[0]])
            dma("gpsimd", "sst", dbg[:, 1536:1537], rho[:], [Bdbg], [Brho], allow_slow_non_contiguous=True)
            dma("gpsimd", "sst", dbg[:, 1537:1538], ahp[:], [Bdbg], [Bahp], allow_slow_non_contiguous=True)
            dma("gpsimd", "sst", dbg[:, 1538:1539], rwp[:], [Bdbg], [Brwp], allow_slow_non_contiguous=True)
            dma("gpsimd", "sst", dbg[:, 1539:1540], am[:], [Bdbg], [Bam], allow_slow_non_contiguous=True)
            dma("gpsimd", "sst", dbg[:, 1540:1541], wscp[:], [Bdbg], [Bwscp],
                allow_slow_non_contiguous=True)

        def ffn(final):
            ffn_calls[0] += 1
            axmx = af32[:, 0:C]
            axmn = af32[:, C:2 * C]
            op("vector", "memset", [Baf32], [], axmx[:, :], -3.0e38)
            op("vector", "memset", [Baf32], [], axmn[:, :], 3.0e38)
            for c in range(NCH):
                sl = slice(c * C, (c + 1) * C)
                for f in range(NF):
                    dma("sync", "sld", xc[:, f * C:(f + 1) * C],
                        xw[f * 128:(f + 1) * 128, sl], [BxcS[f]], [Bxw])
                ln_chunk_stats(extra_rms=True)
                dma("gpsimd", "sst", rows_h[0:1, sl], m_r[:], [Browsh], [Bm_r])
                dma("gpsimd", "sst", rows_h[1:2, sl], r_r[:], [Browsh], [Br_r])
                bcast_row(m_r[:], Bm_r, mb[:], Bmb)
                bcast_row(r_r[:], Br_r, rb[:], Brb)
                for f in range(NF):
                    op("vector", "tensor_tensor", [Bt1], [BxcS[f], Bmb], t1[:],
                       xc[:, f * C:(f + 1) * C], mb[:], OP.subtract)
                    op("vector", "scalar_tensor_tensor", [Bt3], [Bt1, Bgcol, Brb],
                       t3[:], t1[:], gcol[:, f:f + 1], rb[:], OP.mult, OP.mult)
                    op("vector", "tensor_tensor", [Baf32], [Baf32, Bt3],
                       axmx[:, :], axmx[:, :], t3[:], OP.max)
                    op("vector", "tensor_tensor", [Baf32], [Baf32, Bt3],
                       axmn[:, :], axmn[:, :], t3[:], OP.min)
            tree(af32, Baf32, 128, 0, C, OP.max)
            tree(af32, Baf32, 128, C, C, OP.min)
            op("vector", "tensor_scalar", [Btmpc], [Baf32], tmpc[:],
               af32[:, C:C + 1], -1.0, None, OP.mult)
            op("vector", "tensor_tensor", [Bmxc], [Baf32, Btmpc], mxc[:],
               af32[:, 0:1], tmpc[:], OP.max)
            dma("gpsimd", "sst", row_h[0:1, 0:128], mxc[:, 0:1], [Browh], [Bmxc])
            dma("gpsimd", "sst", scrc[:, 0:128],
                row_h[0:1, 0:128].broadcast_to((128, 128)), [Bscrc], [Browh])
            tree(scrc, Bscrc, 128, 0, 128, OP.max)
            op("vector", "tensor_copy", [Bmxc], [Bscrc], mxc[:], scrc[:, 0:1])
            dma("gpsimd", "sst", cc_in[0:1, :], mxc[0:8, 0:1], [Bccin], [Bmxc])
            p.collective([Bccout], [Bccin], "AllReduce", OP.max,
                         replica_groups=[list(range(8))],
                         ins=[cc_in.ap().opt()], outs=[cc_out.ap().opt()])
            dma("gpsimd", "sst", scp[:],
                cc_out[0:1, 0:1].broadcast_to((128, 1)), [Bscp], [Bccout])
            op("vector", "tensor_scalar", [Btmpc], [Bscp], tmpc[:], scp[:],
               1e-5, None, OP.max)
            op("vector", "tensor_copy", [Bscp], [Btmpc], scp[:], tmpc[:])
            op("vector", "reciprocal", [Btmpc], [Bscp], tmpc[:], scp[:])
            op("vector", "tensor_scalar", [Brho], [Btmpc], rho[:], tmpc[:],
               127.0, None, OP.mult)
            op("vector", "tensor_scalar", [Btmpc], [Bscp], tmpc[:], scp[:],
               0.5 / 127.0, None, OP.mult)
            op("vector", "tensor_tensor", [Bahp], [Btmpc, Bwscp], ahp[:],
               tmpc[:], wscp[:], OP.mult)
            if DEBUG and ffn_calls[0] == 1:
                Bdbg = Buf()
                dma("gpsimd", "sst", dbg[:, 1543:1544], scp[:], [Bdbg], [Bscp],
                    allow_slow_non_contiguous=True)
                dma("gpsimd", "sst", dbg[:, 1544:1545], rho[:], [Bdbg], [Brho],
                    allow_slow_non_contiguous=True)

            for c in range(NCH):
                sl = slice(c * C, (c + 1) * C)
                for f in range(NF):
                    dma("sync", "sld", xc[:, f * C:(f + 1) * C],
                        xw[f * 128:(f + 1) * 128, sl], [BxcS[f]], [Bxw])
                dma("sync", "sld", m_r[:], rows_h[0:1, sl], [Bm_r], [Browsh])
                dma("sync", "sld", r_r[:], rows_h[1:2, sl], [Br_r], [Browsh])
                bcast_row(m_r[:], Bm_r, mb[:], Bmb)
                bcast_row(r_r[:], Br_r, rb[:], Brb)
                op("vector", "tensor_scalar", [Brb], [Brb, Brho], rb[:], rb[:],
                   rho[:], None, OP.mult)
                for f in range(NF):
                    op("vector", "tensor_tensor", [Bt1], [BxcS[f], Bmb], t1[:],
                       xc[:, f * C:(f + 1) * C], mb[:], OP.subtract)
                    op("vector", "scalar_tensor_tensor", [Bt3], [Bt1, Bgcol, Brb],
                       t3[:], t1[:], gcol[:, f:f + 1], rb[:], OP.mult, OP.mult)
                    op("vector", "tensor_scalar", [Bt3], [Bt3], t3[:], t3[:],
                       -128.0, 127.0, OP.max, OP.min)
                    round_bf16(t3[:], Bt3, rnd[:, f * C:(f + 1) * C], BaS[f], C)
                for o in range(NF):
                    par = o % 2
                    dma("sync", "sld",
                        wbhi[par][:].rearrange("p (c j) -> p c j", j=128),
                        wqTs[:, o * 128:(o + 1) * 128].rearrange(
                            "(kc pq) jj -> pq kc jj", pq=128),
                        [Bwbhi[par]], [BwqTs])
                    for kc in range(NF):
                        mm([Bpp[par]], [Bwbhi[par], BaS[kc]], pp[par][:],
                           wbhi[par][:, kc * 128:(kc + 1) * 128],
                           rnd[:, kc * C:(kc + 1) * C],
                           start=(kc == 0), stop=(kc == NF - 1))
                    op("vector", "scalar_tensor_tensor", [BxcS[o]],
                       [Bpp[par], Bahp, BxcS[o]], xc[:, o * C:(o + 1) * C],
                       pp[par][:], ahp[:], xc[:, o * C:(o + 1) * C],
                       OP.mult, OP.add)
                    ffn_dbg(c, o)
                if not final:
                    for f in range(NF):
                        dma("gpsimd", "sst", xw[f * 128:(f + 1) * 128, sl],
                            xc[:, f * C:(f + 1) * C], [Bxw], [BxcS[f]])
                else:
                    for f in range(NF):
                        for ts_ in range(4):
                            par2 = ts_ % 2
                            tr([Bpp[par2]], [BxcS[f], Bidt], pp[par2][:, 0:128],
                               xc[:, f * C + ts_ * 128: f * C + (ts_ + 1) * 128],
                               idt[:])
                            op("vector", "tensor_copy", [Bstage], [Bpp[par2]],
                               stage[:, ts_ * 128:(ts_ + 1) * 128],
                               pp[par2][:, 0:128])
                        for ts_ in range(4):
                            dma("gpsimd", "sst",
                                yout[c * C + ts_ * 128: c * C + (ts_ + 1) * 128,
                                     f * 128:(f + 1) * 128],
                                stage[:, ts_ * 128:(ts_ + 1) * 128], [By],
                                [Bstage])

        if "ffn1" in PHASES:
            ffn(final=False)

        if "attn" in PHASES:
            op("vector", "memset", [Bkmax2], [], kmax2[:], 0.0)
            for qc in range(ATTN_NCH):
                sl = slice(qc * C, (qc + 1) * C)
                for f in range(NF):
                    dma("sync", "sld", xc[:, f * C:(f + 1) * C],
                        xw[f * 128:(f + 1) * 128, sl], [BxcS[f]], [Bxw])
                ln_chunk_stats(extra_rms=False)
                bcast_row(m_r[:], Bm_r, mb[:], Bmb)
                bcast_row(r_r[:], Br_r, rb[:], Brb)
                for f in range(NF):
                    op("vector", "tensor_tensor", [Bt1], [BxcS[f], Bmb], t1[:],
                       xc[:, f * C:(f + 1) * C], mb[:], OP.subtract)
                    op("vector", "tensor_tensor", [Bt3], [Bt1, Brb], t3[:],
                       t1[:], rb[:], OP.mult)
                    split_hilo(t3[:], Bt3, xnhi[:, f * C:(f + 1) * C], BaS[f],
                               xnlo[:, f * C:(f + 1) * C], BbS[f])
                for o in range(NPAIR):
                    par = o % 2
                    for (wsrc, Bwd, wb_, Bwb_) in (
                            (wqe["hi"], Bwqe["hi"], wbhi[par], Bwbhi[par]),
                            (wqe["lo"], Bwqe["lo"], wblo[par], Bwblo[par])):
                        dma("sync", "sld",
                            wb_[:].rearrange("p (c j) -> p c j", j=128),
                            wsrc[:, o * 128:(o + 1) * 128].rearrange(
                                "(kc pq) jj -> pq kc jj", pq=128),
                            [Bwb_], [Bwd])
                    for kc in range(NF):
                        ksl = slice(kc * 128, (kc + 1) * 128)
                        xsl = slice(kc * C, (kc + 1) * C)
                        mm([Bpp[par]], [Bwbhi[par], BaS[kc]], pp[par][:],
                           wbhi[par][:, ksl], xnhi[:, xsl],
                           start=(kc == 0), stop=False)
                        mm([Bpp[par]], [Bwbhi[par], BbS[kc]], pp[par][:],
                           wbhi[par][:, ksl], xnlo[:, xsl], start=False,
                           stop=False)
                        mm([Bpp[par]], [Bwblo[par], BaS[kc]], pp[par][:],
                           wblo[par][:, ksl], xnhi[:, xsl], start=False,
                           stop=(kc == NF - 1))
                    op("vector", "tensor_scalar", [Bscra], [Bpp[par], Bbqc],
                       scra[:], pp[par][:], bqc[:, o:o + 1], None, OP.add)
                    split_hilo(scra[:], Bscra, qhi[:, o * C:(o + 1) * C], BqS[o],
                               qlo[:, o * C:(o + 1) * C], BqloS[o])
                    op("vector", "tensor_tensor", [Bscrb], [Bscra], scrb[:],
                       scra[:], scra[:], OP.mult)
                    mm([Bpms], [Bscrb, Bones33], pms[0:33, :], ones33[:],
                       scrb[:], start=True, stop=True)
                    op("vector", "tensor_copy", [Bscrb], [Bpms], scrb[0:33, :],
                       pms[0:33, :])
                    tree(scrb, Bscrb, 33, 0, C, OP.max)
                    op("vector", "tensor_copy", [Bqm2], [Bscrb],
                       qm2[:, o:o + 1], scrb[0:33, 0:1])
                for o in range(8):
                    par = o % 2
                    for (wsrc, Bwd, wb_, Bwb_) in (
                            (wls["hi"], Bwls["hi"], wbhi[par], Bwbhi[par]),
                            (wls["lo"], Bwls["lo"], wblo[par], Bwblo[par])):
                        dma("sync", "sld",
                            wb_[:].rearrange("p (c j) -> p c j", j=128),
                            wsrc[:, o * 128:(o + 1) * 128].rearrange(
                                "(kc pq) jj -> pq kc jj", pq=128),
                            [Bwb_], [Bwd])
                    for kc in range(NF):
                        ksl = slice(kc * 128, (kc + 1) * 128)
                        xsl = slice(kc * C, (kc + 1) * C)
                        mm([Bpp[par]], [Bwbhi[par], BaS[kc]], pp[par][:],
                           wbhi[par][:, ksl], xnhi[:, xsl],
                           start=(kc == 0), stop=False)
                        mm([Bpp[par]], [Bwbhi[par], BbS[kc]], pp[par][:],
                           wbhi[par][:, ksl], xnlo[:, xsl], start=False,
                           stop=False)
                        mm([Bpp[par]], [Bwblo[par], BaS[kc]], pp[par][:],
                           wblo[par][:, ksl], xnhi[:, xsl], start=False,
                           stop=(kc == NF - 1))
                    op("scalar", "activation", [Bscra], [Bpp[par], Bblc],
                       scra[:], pp[par][:], AF.Silu, bias=blc[:, o:o + 1],
                       scale=1.0)
                    lsl = slice(par * C, (par + 1) * C)
                    split_hilo(scra[:], Bscra, lathi[:, lsl], BlatS[par],
                               latlo[:, lsl], BlatloS[par])
                    dma("sync", "sld", kvwh[:, 0:256],
                        wks["hi"][o * 128:(o + 1) * 128, :], [Bkvwh], [Bwks["hi"]])
                    dma("sync", "sld", kvwh[:, 256:512],
                        wvs["hi"][o * 128:(o + 1) * 128, :], [Bkvwh], [Bwvs["hi"]])
                    dma("sync", "sld", kvwl[:, 0:256],
                        wks["lo"][o * 128:(o + 1) * 128, :], [Bkvwl], [Bwks["lo"]])
                    dma("sync", "sld", kvwl[:, 256:512],
                        wvs["lo"][o * 128:(o + 1) * 128, :], [Bkvwl], [Bwvs["lo"]])
                    for tgt in range(4):
                        pt = (psA[0], psA[1], psB[0], psB[1])[tgt]
                        Bpt = (BpsA[0], BpsA[1], BpsB[0], BpsB[1])[tgt]
                        wsl = slice(tgt * 128, (tgt + 1) * 128)
                        mm([Bpt], [Bkvwh, BlatS[par]], pt[:], kvwh[:, wsl],
                           lathi[:, lsl], start=(o == 0), stop=False)
                        mm([Bpt], [Bkvwh, BlatloS[par]], pt[:], kvwh[:, wsl],
                           latlo[:, lsl], start=False, stop=False)
                        mm([Bpt], [Bkvwl, BlatS[par]], pt[:], kvwl[:, wsl],
                           lathi[:, lsl], start=False, stop=(o == 7))
                for o2 in range(2):
                    op("vector", "tensor_scalar", [Bscra], [BpsA[o2], Bbkc],
                       scra[:], psA[o2][:], bkc[:, o2:o2 + 1], None, OP.add)
                    split_hilo(scra[:], Bscra,
                               khi[:, o2 * T + qc * C: o2 * T + (qc + 1) * C],
                               Bkhi,
                               klo[:, o2 * T + qc * C: o2 * T + (qc + 1) * C],
                               Bklo)
                    op("vector", "tensor_tensor", [Bscrb], [Bscra], scrb[:],
                       scra[:], scra[:], OP.mult)
                    mm([Bpms], [Bscrb, Bones33], pms[0:33, :], ones33[:],
                       scrb[:], start=True, stop=True)
                    op("vector", "tensor_copy", [Bscrb], [Bpms], scrb[0:33, :],
                       pms[0:33, :])
                    tree(scrb, Bscrb, 33, 0, C, OP.max)
                    op("vector", "tensor_tensor", [Bkmax2], [Bkmax2, Bscrb],
                       kmax2[:, o2:o2 + 1], kmax2[:, o2:o2 + 1],
                       scrb[0:33, 0:1], OP.max)
                for o2 in range(2):
                    op("vector", "tensor_scalar", [Bscra], [BpsB[o2], Bbvc],
                       scra[:], psB[o2][:], bvc[:, o2:o2 + 1], None, OP.add)
                    split_hilo(scra[:], Bscra, wbf[:, 0:C], Bwbf,
                               wbf2[:, 0:C], Bwbf2)
                    for ts_ in range(4):
                        tile = qc * 4 + ts_
                        for hl in range(2):
                            src = wbf if hl == 0 else wbf2
                            Bsrc = Bwbf if hl == 0 else Bwbf2
                            dst = vhi if hl == 0 else vlo
                            Bdst = Bvhi if hl == 0 else Bvlo
                            op("vector", "tensor_copy", [Bscrb], [Bsrc],
                               scrb[:, 0:128], src[:, ts_ * 128:(ts_ + 1) * 128])
                            tr([Bpp[0]], [Bscrb, Bidt], pp[0][:, 0:128],
                               scrb[:, 0:128], idt[:])
                            op("vector", "tensor_copy", [Bdst], [Bpp[0]],
                               dst[:, tile * 256 + o2 * 128:
                                   tile * 256 + (o2 + 1) * 128],
                               pp[0][:, 0:128])
                for j in range(ATTN_NPAIR if ATTN_INNER else 0):
                    kch = j // 4
                    ga = 2 * kch
                    op("vector", "tensor_tensor", [Bbpair], [Bqm2, Bkmax2],
                       bpair[:], qm2[:, j:j + 1], kmax2[:, kch:kch + 1], OP.mult)
                    op("scalar", "activation", [Bbpair], [Bbpair], bpair[:],
                       bpair[:], AF.Sqrt)
                    op("vector", "tensor_scalar", [Bbpair], [Bbpair], bpair[:],
                       bpair[:], -0.125, None, OP.mult)
                    tr([BpsB[0]], [Bbpair, Bidt], psB[0][0:1, 0:33],
                       bpair[0:33, :], idt[0:33, 0:33])
                    op("vector", "tensor_copy", [BbnegT], [BpsB[0]], bnegT[:],
                       psB[0][0:1, 0:33])
                    mm([BpsB[0]], [BbnegT, Bonesr], psB[0][:, 0:1], onesr[:],
                       bnegT[0:1, 0:1], start=True, stop=True)
                    op("vector", "tensor_copy", [BbiasA], [BpsB[0]], biasA[:],
                       psB[0][:, 0:1])
                    mm([BpsB[0]], [BbnegT, Bonesr], psB[0][:, 0:1], onesr[:],
                       bnegT[0:1, 32:33], start=True, stop=True)
                    op("vector", "tensor_copy", [BbiasB], [BpsB[0]], biasB[:],
                       psB[0][:, 0:1])
                    nkt = 4 * qc + 4
                    for kt in range(nkt if ATTN_LVL >= 2 else 0):
                        par = kt % 2
                        off = 0 if kt < 4 * qc else (kt - 4 * qc) * 128
                        w_ = C - off
                        ktsl = slice(kch * T + kt * 128, kch * T + (kt + 1) * 128)
                        qsl = slice(j * C + off, (j + 1) * C)
                        esl = slice(0, w_)
                        mm([BpsA[par]], [Bkhi, BqS[j]], psA[par][:, esl],
                           khi[0:64, ktsl], qhi[0:64, qsl], start=True, stop=False)
                        mm([BpsA[par]], [Bkhi, BqloS[j]], psA[par][:, esl],
                           khi[0:64, ktsl], qlo[0:64, qsl], start=False,
                           stop=False)
                        mm([BpsA[par]], [Bklo, BqS[j]], psA[par][:, esl],
                           klo[0:64, ktsl], qhi[0:64, qsl], start=False,
                           stop=True)
                        mm([BpsB[par]], [Bkhi, BqS[j]], psB[par][:, esl],
                           khi[64:128, ktsl], qhi[64:128, qsl], start=True,
                           stop=False)
                        mm([BpsB[par]], [Bkhi, BqloS[j]], psB[par][:, esl],
                           khi[64:128, ktsl], qlo[64:128, qsl], start=False,
                           stop=False)
                        mm([BpsB[par]], [Bklo, BqS[j]], psB[par][:, esl],
                           klo[64:128, ktsl], qhi[64:128, qsl], start=False,
                           stop=True)
                        for hd2 in range(2 if ATTN_LVL >= 3 else 0):
                            pse = psA[par] if hd2 == 0 else psB[par]
                            Bpse = BpsA[par] if hd2 == 0 else BpsB[par]
                            bias_ = biasA if hd2 == 0 else biasB
                            Bbias_ = BbiasA if hd2 == 0 else BbiasB
                            ei = par * 2 + hd2
                            op("scalar", "activation", [Bef[ei]], [Bpse, Bbias_],
                               ef[ei][:, esl], pse[:, esl], AF.Exp,
                               bias=bias_[:], scale=0.125)
                            if kt >= 4 * qc:
                                op("vector", "tensor_tensor", [Bef[ei]],
                                   [Bef[ei], Bm0], ef[ei][:, 0:128],
                                   ef[ei][:, 0:128], m0s[:], OP.mult)
                            op("vector", "tensor_copy", [Behi[ei]], [Bef[ei]],
                               ehi[ei][:, esl], ef[ei][:, esl])
                            op("vector", "tensor_tensor", [Belo[ei]],
                               [Bef[ei], Behi[ei]], elo[ei][:, esl],
                               ef[ei][:, esl], ehi[ei][:, esl], OP.subtract)
                            if ATTN_LVL < 4:
                                continue
                            prs = pms if hd2 == 0 else pp[0]
                            Bprs = Bpms if hd2 == 0 else Bpp[0]
                            mm([Bprs], [Behi[ei], Bonescbf],
                               prs[0:1, off:C], onescbf[:],
                               ehi[ei][:, esl], start=(kt == 0), stop=False)
                            mm([Bprs], [Belo[ei], Bonescbf],
                               prs[0:1, off:C], onescbf[:],
                               elo[ei][:, esl], start=False,
                               stop=(kt == nkt - 1))
                            if ATTN_LVL < 5:
                                continue
                            g = ga + hd2
                            vt = slice(kt * 256 + g * 64, kt * 256 + (g + 1) * 64)
                            ob = hd2 * 64
                            tp = (0, ob)
                            mm([Bpctx], [Bvhi, Behi[ei]], pctx[ob:ob + 64, off:C],
                               vhi[:, vt], ehi[ei][:, esl],
                               start=(kt == 0), stop=False, tile_position=tp)
                            mm([Bpctx], [Bvhi, Belo[ei]], pctx[ob:ob + 64, off:C],
                               vhi[:, vt], elo[ei][:, esl],
                               start=False, stop=False, tile_position=tp)
                            mm([Bpctx], [Bvlo, Behi[ei]], pctx[ob:ob + 64, off:C],
                               vlo[:, vt], ehi[ei][:, esl],
                               start=False, stop=(kt == nkt - 1),
                               tile_position=tp)
                    if ATTN_LVL < 6:
                        continue
                    op("vector", "tensor_copy", [Brs0], [Bpms], rs0[:],
                       pms[0:1, :])
                    op("vector", "tensor_copy", [Brs1], [Bpp[0]], rs1[:],
                       pp[0][0:1, :])
                    op("vector", "reciprocal", [Brs0], [Brs0], rs0[:], rs0[:])
                    op("vector", "reciprocal", [Brs1], [Brs1], rs1[:], rs1[:])
                    mm([BpsA[0]], [Brs0, BmaskA], psA[0][:], maskA[:],
                       rs0[:], start=True, stop=False)
                    mm([BpsA[0]], [Brs1, BmaskB], psA[0][:], maskB[:],
                       rs1[:], start=False, stop=True)
                    op("vector", "tensor_copy", [Brbq], [BpsA[0]], rbq[:],
                       psA[0][:])
                    op("vector", "tensor_tensor", [Bscra], [Bpctx, Brbq],
                       scra[:], pctx[:], rbq[:], OP.mult)
                    split_hilo(scra[:], Bscra, ctxhi[:, j * C:(j + 1) * C],
                               BbS[j], ctxlo[:, j * C:(j + 1) * C], BbS[8 + j])
                for o in range(NF if ATTN_WO else 0):
                    par = o % 2
                    for (wsrc, Bwd, wb_, Bwb_) in (
                            (woe["hi"], Bwoe["hi"], wbhi[par], Bwbhi[par]),
                            (woe["lo"], Bwoe["lo"], wblo[par], Bwblo[par])):
                        dma("sync", "sld",
                            wb_[:, 0:8 * 128].rearrange("p (c j) -> p c j", j=128),
                            wsrc[:, o * 128:(o + 1) * 128].rearrange(
                                "(kc pq) jj -> pq kc jj", pq=128),
                            [Bwb_], [Bwd])
                    for kc in range(8):
                        ksl = slice(kc * 128, (kc + 1) * 128)
                        xsl = slice(kc * C, (kc + 1) * C)
                        mm([Bpp[par]], [Bwbhi[par], BbS[kc]], pp[par][:],
                           wbhi[par][:, ksl], ctxhi[:, xsl],
                           start=(kc == 0), stop=False)
                        mm([Bpp[par]], [Bwbhi[par], BbS[8 + kc]], pp[par][:],
                           wbhi[par][:, ksl], ctxlo[:, xsl], start=False,
                           stop=False)
                        mm([Bpp[par]], [Bwblo[par], BbS[kc]], pp[par][:],
                           wblo[par][:, ksl], ctxhi[:, xsl], start=False,
                           stop=(kc == 7))
                    op("vector", "tensor_scalar", [Bscra], [Bpp[par], Bboc],
                       scra[:], pp[par][:], boc[:, o:o + 1], None, OP.add)
                    op("vector", "tensor_tensor", [BxcS[o]], [Bscra, BxcS[o]],
                       xc[:, o * C:(o + 1) * C], xc[:, o * C:(o + 1) * C],
                       scra[:], OP.add)
                for f in range(NF):
                    dma("gpsimd", "sst", xw[f * 128:(f + 1) * 128, sl],
                        xc[:, f * C:(f + 1) * C], [Bxw], [BxcS[f]])

        if "ffn2" in PHASES:
            ffn(final=("exit" in PHASES))

        if "ffn2" not in PHASES:
            for c in range(NCH):
                for f in range(NF):
                    dma("sync", "sld", xc[:, f * C:(f + 1) * C],
                        xw[f * 128:(f + 1) * 128, c * C:(c + 1) * C],
                        [BxcS[f]], [Bxw])
                for f in range(NF):
                    for ts_ in range(4):
                        par2 = ts_ % 2
                        tr([Bpp[par2]], [Bxc, Bidt], pp[par2][:, 0:128],
                           xc[:, f * C + ts_ * 128: f * C + (ts_ + 1) * 128],
                           idt[:])
                        op("vector", "tensor_copy", [Bstage], [Bpp[par2]],
                           stage[:, ts_ * 128:(ts_ + 1) * 128], pp[par2][:, 0:128])
                    for ts_ in range(4):
                        dma("gpsimd", "sst",
                            yout[c * C + ts_ * 128: c * C + (ts_ + 1) * 128,
                                 f * 128:(f + 1) * 128],
                            stage[:, ts_ * 128:(ts_ + 1) * 128], [By], [Bstage])

        def _fin(e):
            for s in ("sld", "sst"):
                if p.cnt.get(s):
                    e.wait_ge(p.sems[s], p.cnt[s])
        p.raw("gpsimd", _fin)

        for s, v in p.cnt.items():
            assert v < 60000, f"semaphore {s} overflow risk: {v}"

        with nc.Block() as block:
            @block.sync
            def _(eng):
                for fn in p.ops["sync"]:
                    fn(eng)

            @block.tensor
            def _(eng):
                p._pad_src("tensor")
                for i, fn in enumerate(p.ops["tensor"]):
                    try:
                        fn(eng)
                    except Exception:
                        print("TENSOR OP FAIL at", i, p.src["tensor"][i] if i < len(p.src["tensor"]) else "?")
                        raise

            @block.vector
            def _(eng):
                for fn in p.ops["vector"]:
                    fn(eng)

            @block.scalar
            def _(eng):
                for fn in p.ops["scalar"]:
                    fn(eng)

            @block.gpsimd
            def _(eng):
                for fn in p.ops["gpsimd"]:
                    fn(eng)

    return nc


_NC_CACHE = [None]


def kernel(**inputs):
    x = np.ascontiguousarray(np.asarray(inputs["L0_attn_x"], dtype=np.float32))
    f32 = lambda a: np.ascontiguousarray(np.asarray(a, dtype=np.float32))
    base = {
        "Wq": f32(inputs["L0_attn_Wq"]),
        "bq": f32(inputs["L0_attn_bq"]).reshape(D, 1),
        "Wl": f32(inputs["L0_attn_Wl"]),
        "bl": f32(inputs["L0_attn_bl"]).reshape(1, LAT),
        "Wk": f32(inputs["L0_attn_Wk"]),
        "bk": f32(inputs["L0_attn_bk"]).reshape(1, NKV * KD),
        "Wv": f32(inputs["L0_attn_Wv"]),
        "bv": f32(inputs["L0_attn_bv"]).reshape(1, NKV * KD),
        "Wq2kv": f32(inputs["L0_attn_Wq2kv"]),
        "Wkv2h": f32(inputs["L0_attn_Wkv2h"]),
        "bkv2h": f32(inputs["L0_attn_bkv2h"]).reshape(HD, 1),
        "Wo": f32(inputs["L0_attn_Wo"]),
        "bo": f32(inputs["L0_attn_bo"]).reshape(1, D),
        "Wf": f32(inputs["L0_ffn_weight"]),
        "gam": f32(inputs["L0_ffn_norm_weight"]).reshape(D, 1),
        "ident": np.eye(128, dtype=np.float32),
        "m0": np.triu(np.ones((128, 128), dtype=np.float32)),
    }
    if _NC_CACHE[0] is None:
        _NC_CACHE[0] = build_nc()
    nc = _NC_CACHE[0]
    in_maps = [dict(base, x_in=x[c]) for c in range(8)]
    res = run_bass_kernel_spmd(nc, in_maps, core_ids=list(range(8)))
    if DEBUG:
        kernel.dbg = [np.asarray(r["dbg"], dtype=np.float32) for r in res.results]
    return np.stack([np.asarray(r["y"], dtype=np.float32) for r in res.results])
